# version 02
# baseline (speedup 1.0000x reference)
import numpy as np
import ml_dtypes
import concourse.bass as bass
import concourse.mybir as mybir
from concourse.bass_utils import run_bass_kernel_spmd

F32 = mybir.dt.float32
BF16 = mybir.dt.bfloat16
I32 = mybir.dt.int32
AX = mybir.AxisListType.X
OP = mybir.AluOpType
AF = mybir.ActivationFunctionType

T = 2048
D = 1024
NCH = D // 128
NT = T // 512


ALL_SEMS = []

class Sem:
    def __init__(self, nc, name):
        self.h = nc.alloc_semaphore(name=name)
        self.n = 0
        ALL_SEMS.append(self.h)

    def bump(self, k=1):
        self.n += k
        return self.n


class Prog:
    ENGINES = ("sync", "gpsimd", "vector", "scalar", "tensor")

    def __init__(self, nc):
        self.nc = nc
        self.ops = {e: [] for e in self.ENGINES}
        self.semT = Sem(nc, "semT")
        self.semV = Sem(nc, "semV")
        self.semS = Sem(nc, "semS")
        self.semG = Sem(nc, "semG")
        self.semD = Sem(nc, "semD")
        self.semA = Sem(nc, "semA")
        self.esem = dict(tensor=self.semT, vector=self.semV, scalar=self.semS)
        self.dsem = dict(gpsimd=self.semG, sync=self.semD, scalar=self.semA)
        self.bankV = [Sem(nc, f"bk{i}") for i in range(8)]
        self.bank_fills = [0] * 8
        self.bar = Sem(nc, "bar")
        self.barg = {e: Sem(nc, f"barg_{e}") for e in ("tensor", "vector", "scalar")}
        self.dma_group_sems = [self.semG, self.semD, self.semA]
        self._groups = {}

    def op(self, eng, fn, inc=False, inc_sem=None, inc_by=1):
        sems = []
        ret = None
        if inc:
            s = self.esem[eng]
            ret = s.bump(1)
            sems.append((s.h, 1))
        if inc_sem is not None:
            v = inc_sem.bump(inc_by)
            sems.append((inc_sem.h, inc_by))
            if ret is None:
                ret = v

        def run(e, fn=fn, sems=tuple(sems), eng=eng):
            ins = fn(e)
            if sems:
                ins = ins.then_inc(*sems[0])
                for h, k in sems[1:]:
                    self._marker(e, eng).then_inc(h, k)
            return ins
        self.ops[eng].append(run)
        return ret

    def _marker(self, e, eng):
        if eng == "vector":
            return e.tensor_copy(self.tinyV[:], self.tinyV[:])
        if eng == "scalar":
            return e.copy(self.tinyS[:], self.tinyS[:])
        if eng == "gpsimd":
            return e.tensor_copy(self.tinyG[:], self.tinyG[:])
        return e.nop()

    def mset(self, eng, fn):
        v = self.op(eng, fn, inc=True)
        self.wait(eng, self.esem[eng], v)
        return v

    def dma(self, eng, fn, sem=None):
        s = sem if sem is not None else self.dsem[eng]
        val = s.bump(16)
        self.ops[eng].append(lambda e, fn=fn, h=s.h: fn(e).then_inc(h, 16))
        return val

    def fence(self, eng, qsem):
        self.wait(eng, qsem, qsem.n)

    def gsem(self, name):
        if name not in self._groups:
            self._groups[name] = Sem(self.nc, name)
            self.dma_group_sems.append(self._groups[name])
        return self._groups[name]

    def wait(self, eng, sem, val):
        if val:
            self.ops[eng].append(lambda e, h=sem.h, v=val: e.wait_ge(h, v))

    def claim_bank(self, i):
        self.wait("tensor", self.bankV[i], self.bank_fills[i])

    def evac_done(self, i):
        self.bank_fills[i] = self.bankV[i].n

    def barrier(self, engines=ENGINES):
        for e in ("tensor", "vector", "scalar"):
            bg = self.barg[e]
            v = bg.bump(1)
            self.ops[e].append(lambda h, hh=bg.h, eng=e: self._marker(h, eng).then_inc(hh, 1))
            self.wait("gpsimd", bg, v)
        for s in self.dma_group_sems:
            if s.n > 0:
                self.wait("gpsimd", s, s.n)
        g = self.bar.bump(1)
        self.ops["gpsimd"].append(lambda e, h=self.bar.h: e.nop().then_inc(h, 1))
        for e in self.ENGINES:
            self.wait(e, self.bar, g)

    def replay(self):
        nc = self.nc
        nums = sorted(h.num for h in ALL_SEMS)
        with nc.Block() as b0:
            @b0.gpsimd
            def _(g):
                lo, hi = nums[0], nums[-1]
                assert nums == list(range(lo, hi + 1)), nums
                g.dma_reset(range(lo, hi + 1))
                g.sem_clear(range(lo, hi + 1))
        with nc.Block() as block:
            @block.sync
            def _(eng):
                for f in self.ops["sync"]:
                    f(eng)

            @block.gpsimd
            def _(eng):
                for f in self.ops["gpsimd"]:
                    f(eng)

            @block.vector
            def _(eng):
                for f in self.ops["vector"]:
                    f(eng)

            @block.scalar
            def _(eng):
                for f in self.ops["scalar"]:
                    f(eng)

            @block.tensor
            def _(eng):
                for f in self.ops["tensor"]:
                    f(eng)



WSPECS = [
    ("mu0", (128, 8)), ("W_x0", (D, 160)), ("W_x2", (5, 32, D)), ("x_bias", (128, 40)),
    ("W_r", (D, 512)), ("W_w1", (D, 64)), ("W_w2", (64, 512)), ("b_w2", (128, 4)),
    ("W_k", (D, 512)), ("W_v", (D, D)), ("W_g", (D, D)), ("bonus", (128, 4)),
    ("gn_w", (1, D)), ("gn_b", (1, D)), ("W_o", (D, D)),
    ("w1", (D, 4096)), ("conv_w", (128, 32, 3)), ("conv_b", (128, 32)), ("w2", (4096, D)),
    ("W_in", (D, 2080)), ("ck", (128, 17, 4)), ("theta", (128, 8)), ("slopes", (32, 1)),
    ("score_scale", (32, 1)), ("W_re", (32, 32)), ("W_im", (32, 32)),
    ("ns_re", (128, 1)), ("ns_im", (128, 1)), ("W_out", (D, D)),
    ("router_W", (D, 16)), ("router_b", (1, 15)), ("sW1", (D, D)),
    ("sW2", (D, D)), ("eW1", (15, D, D)), ("eW2", (15, D, D)),
    ("eb1sb1", (16, 128, 8)), ("eb2sb2", (16, D)),
]


def host_layout(inputs):
    g = {}
    f32 = lambda a: np.ascontiguousarray(np.asarray(a), dtype=np.float32)
    g["mu0"] = np.ascontiguousarray(f32(inputs["L0_attn_mu0"]).reshape(8, 128).T)
    g["W_x0"] = np.ascontiguousarray(f32(inputs["L0_attn_W_x0"]).T)
    g["W_x2"] = np.ascontiguousarray(f32(inputs["L0_attn_W_x2"]).reshape(D, 5, 32).transpose(1, 2, 0))
    g["x_bias"] = np.ascontiguousarray(f32(inputs["L0_attn_x_bias"]).reshape(5, 8, 128).transpose(2, 0, 1).reshape(128, 40))
    g["W_r"] = np.ascontiguousarray(f32(inputs["L0_attn_W_r"]).T)
    g["W_w1"] = np.ascontiguousarray(f32(inputs["L0_attn_W_w1"]).T)
    g["W_w2"] = np.ascontiguousarray(f32(inputs["L0_attn_W_w2"]).T)
    g["b_w2"] = np.ascontiguousarray(f32(inputs["L0_attn_b_w2"]).reshape(4, 128).T)
    g["W_k"] = np.ascontiguousarray(f32(inputs["L0_attn_W_k"]).T)
    g["W_v"] = np.ascontiguousarray(f32(inputs["L0_attn_W_v"]).T)
    g["W_g"] = np.ascontiguousarray(f32(inputs["L0_attn_W_g"]).T)
    g["bonus"] = np.ascontiguousarray(f32(inputs["L0_attn_bonus"]).T)
    g["gn_w"] = f32(inputs["L0_attn_gn_weight"]).reshape(1, D)
    g["gn_b"] = f32(inputs["L0_attn_gn_bias"]).reshape(1, D)
    g["W_o"] = np.ascontiguousarray(f32(inputs["L0_attn_W_o"]).T)
    g["w1"] = np.ascontiguousarray(f32(inputs["L0_ffn_w1"]).T)
    g["conv_w"] = np.ascontiguousarray(f32(inputs["L0_ffn_conv_w"]).reshape(32, 128, 3).transpose(1, 0, 2))
    g["conv_b"] = np.ascontiguousarray(f32(inputs["L0_ffn_conv_b"]).reshape(32, 128).T)
    g["w2"] = np.ascontiguousarray(f32(inputs["L0_ffn_w2"]).T)
    g["W_in"] = f32(inputs["L1_attn_W_in"])
    ckp = np.zeros((2176, 4), np.float32); ckp[:2080] = f32(inputs["L1_attn_conv_kernel"]).reshape(4, 2080).T
    g["ck"] = np.ascontiguousarray(ckp.reshape(17, 128, 4).transpose(1, 0, 2))
    g["theta"] = np.ascontiguousarray(f32(inputs["L1_attn_theta"]).reshape(8, 128).T)
    g["slopes"] = np.concatenate([f32(inputs["L1_attn_decay_slopes"]),
                                  f32(inputs["L1_attn_anchor_slopes"])]).reshape(32, 1)
    g["score_scale"] = f32(inputs["L1_attn_score_scale"]).reshape(32, 1)
    g["W_re"] = f32(inputs["L1_attn_W_re"])
    g["W_im"] = f32(inputs["L1_attn_W_im"])
    ns = f32(inputs["L1_attn_norm_scale"])
    g["ns_re"] = np.ascontiguousarray(np.tile(ns[:32], 4).reshape(128, 1))
    g["ns_im"] = np.ascontiguousarray(np.tile(ns[32:], 4).reshape(128, 1))
    g["W_out"] = f32(inputs["L1_attn_W_out"])
    rwp = np.zeros((D, 16), np.float32)
    rwp[:, 0:15] = f32(inputs["L1_ffn_router_W"])
    g["router_W"] = rwp
    g["router_b"] = f32(inputs["L1_ffn_router_b"]).reshape(1, 15)
    g["sW1"] = f32(inputs["L1_ffn_sW1"])
    g["sW2"] = f32(inputs["L1_ffn_sW2"])
    g["eW1"] = f32(inputs["L1_ffn_eW1"])
    g["eW2"] = f32(inputs["L1_ffn_eW2"])
    eb1 = f32(inputs["L1_ffn_eb1"]).reshape(15, 8, 128).transpose(0, 2, 1)
    sb1 = f32(inputs["L1_ffn_sb1"]).reshape(8, 128).T
    g["eb1sb1"] = np.ascontiguousarray(np.concatenate([eb1, sb1[None]], axis=0))
    g["eb2sb2"] = np.ascontiguousarray(np.concatenate(
        [f32(inputs["L1_ffn_eb2"]), f32(inputs["L1_ffn_sb2"]).reshape(1, D)], axis=0))
    return g



class St:
    pass


def setup(st, debug_names):
    ALL_SEMS.clear()
    nc = bass.Bass("TRN2", target_bir_lowering=False, detect_race_conditions=False)
    st.nc = nc
    st.p = Prog(nc)
    st.debug_names = set(debug_names)
    st.dbg_outs = []

    st.x0 = nc.dram_tensor("x0", [D, T], F32, kind="ExternalInput").ap()

    class LazyW(dict):
        def __missing__(self, name):
            shape = dict(WSPECS)[name]
            ap = nc.dram_tensor(name, list(shape), F32, kind="ExternalInput").ap()
            self[name] = ap
            return ap
    st.W = LazyW()
    st.xout = nc.dram_tensor("xout", [D, T], F32, kind="ExternalOutput").ap()
    st.rscr = nc.dram_tensor("rscr", [16, T], F32, kind="Internal").ap()

    st._sbn = [0]
    from contextlib import ExitStack
    st.regions = [ExitStack()]
    def sb(name, shape, dt=F32):
        st._sbn[0] += 1
        return st.regions[-1].enter_context(
            nc.sbuf_tensor(f"{name}_{st._sbn[0]}", list(shape), dt))
    st.sb = sb
    def push_region():
        st.regions.append(ExitStack())
    def pop_region():
        st.regions.pop().close()
    st.push_region = push_region
    st.pop_region = pop_region
    st.psum = [nc.alloc_psum_tensor(f"ps{i}", [128, 512], F32) for i in range(7)]
    st.psT = nc.alloc_psum_tensor("psT", [128, 1024], BF16)

    p = st.p
    st.ones_row = sb("ones_row", [1, T], BF16)
    st.iota_row = sb("iota_row", [1, T])
    st.zeros_bf = sb("zeros_bf", [128, 512], BF16)
    st.idt_bf = sb("idt_bf", [128, 128], BF16)
    st.mask_sl = sb("mask_sl", [128, 128], BF16)
    st.iota_bc = sb("iota_bc", [128, 128])
    st.pcol = sb("pcol", [128, 1])
    st.ones_col = sb("ones_col", [128, 1])
    p.tinyV = sb("tinyV", [1, 2])
    p.tinyS = sb("tinyS", [1, 2])
    p.tinyG = sb("tinyG", [1, 2])
    st.eps_col = sb("eps_col", [128, 1])
    st.ones_col_bf = sb("ones_col_bf", [128, 1], BF16)
    st.b128_col = sb("b128_col", [128, 1])

    p.mset("vector", lambda v: v.memset(p.tinyV[:], 0.0))
    p.op("scalar", lambda x: x.memzero(p.tinyS[:]))
    p.op("gpsimd", lambda x: x.memset(p.tinyG[:], 0.0))
    p.mset("vector", lambda v: v.memset(st.ones_row[:], 1.0))
    p.mset("vector", lambda v: v.memset(st.zeros_bf[:], 0.0))
    p.mset("vector", lambda v: v.memset(st.ones_col[:], 1.0))
    p.mset("vector", lambda v: v.memset(st.eps_col[:], 1e-5))
    p.mset("vector", lambda v: v.memset(st.ones_col_bf[:], 1.0))
    p.mset("vector", lambda v: v.memset(st.b128_col[:], 128.0))
    vi = p.op("vector", lambda v: v.tensor_tensor_scan(
        st.iota_row[:], st.ones_row[:], st.ones_row[:], -1.0, OP.add, OP.bypass), inc=True)
    p.wait("sync", p.semV, vi)
    sC = p.gsem("sConst")
    p.dma("sync", lambda e: e.dma_start(out=st.rscr[0:1, :], in_=st.iota_row[:]), sem=sC)
    p.fence("sync", sC)
    p.dma("sync", lambda e: e.dma_start(out=st.iota_bc[:], in_=st.rscr[0:1, 0:128].partition_broadcast(128)), sem=sC)
    p.dma("sync", lambda e: e.dma_start(out=st.pcol[:], in_=st.rscr[0:1, 0:128].rearrange("o p -> p o")), sem=sC)
    p.fence("vector", sC)
    p.op("vector", lambda v: v.tensor_scalar(st.idt_bf[:], st.iota_bc[:], st.pcol[:], None, OP.is_equal), inc=True)
    p.op("vector", lambda v: v.tensor_scalar(st.mask_sl[:], st.iota_bc[:], st.pcol[:], None, OP.is_gt), inc=True)
    st.const_v = p.semV.n
    p.wait("vector", p.semV, st.const_v)
    return st


def dbg(st, name, ap_fn, shape, dtype=F32):
    if name not in st.debug_names:
        return
    o = st.nc.dram_tensor("dbg_" + name, list(shape), dtype, kind="ExternalOutput").ap()
    st.dbg_outs.append("dbg_" + name)
    st.p.barrier()
    st.p.dma("sync", lambda e, o=o: e.dma_start(out=o, in_=ap_fn()))


def bcast_row(st, row_ap, out_bc, scr_row_idx, width=T):
    p = st.p
    sB = p.gsem("sBc")
    p.dma("sync", lambda e: e.dma_start(
        out=st.rscr[scr_row_idx:scr_row_idx + 1, 0:width], in_=row_ap), sem=sB)
    p.fence("sync", sB)
    return p.dma("sync", lambda e: e.dma_start(
        out=out_bc, in_=st.rscr[scr_row_idx:scr_row_idx + 1, 0:width].partition_broadcast(128)), sem=sB)



def ph_layernorm(st, src_ap, xn, write_through, wait_dma_counts=()):
    nc, p, sb = st.nc, st.p, st.sb
    psum = st.psum
    import os
    SUB = int(os.environ.get("MK_SUB", "99"))

    st.push_region()
    x_sb = sb("lnx", [128, NCH, T])
    xsq = sb("lnxsq", [128, 2, T])
    mrow = sb("mrow", [1, T])
    vrow = sb("vrow", [1, T])
    rsrow = sb("rsrow", [1, T])
    m_bc = sb("m_bc", [128, T])
    rs_bc = sb("rs_bc", [128, T])

    xr = src_ap.rearrange("(c p) t -> p c t", p=128)
    p.mset("vector", lambda v: v.memset(xn[:, :, 0:1], 0.0))
    p.mset("vector", lambda v: v.memset(xn[:, :, T + 1:T + 2], 0.0))
    sLdx = p.gsem("sLdx")
    ld = []
    for c in range(NCH):
        ld.append(p.dma("sync", lambda e, c=c: e.dma_start(out=x_sb[:, c, :], in_=xr[:, c, :]), sem=sLdx))
    sWx = p.gsem("sWx")
    if write_through:
        xo = st.xout.rearrange("(c p) t -> p c t", p=128)
        p.fence("gpsimd", sLdx)
        for c in range(NCH):
            p.dma("gpsimd", lambda e, c=c: e.dma_start(out=xo[:, c, :], in_=x_sb[:, c, :]), sem=sWx)

    if SUB < 1:
        p.barrier(); st.pop_region(); return
    for nt in range(NT):
        p.claim_bank(nt)
    p.wait("tensor", p.semV, st.const_v)
    p.fence("tensor", sLdx)
    for nt in range(NT):
        for c in range(NCH):
            p.op("tensor", lambda t, c=c, nt=nt: t.matmul(
                psum[nt][0:1, :], st.ones_col[:], x_sb[:, c, nt * 512:(nt + 1) * 512],
                start=(c == 0), stop=(c == NCH - 1)),
                inc=(nt == NT - 1 and c == NCH - 1))
    tmm = p.semT.n
    p.wait("vector", p.semT, tmm)
    for nt in range(NT):
        p.op("vector", lambda v, nt=nt: v.tensor_copy(mrow[0:1, nt * 512:(nt + 1) * 512], psum[nt][0:1, :]),
             inc_sem=p.bankV[nt])
        p.evac_done(nt)

    if SUB < 2:
        p.barrier(); st.pop_region(); return
    st._lnsq_t = {}
    for c in range(NCH):
        p.fence("scalar", sLdx)
        if c >= 2:
            p.wait("scalar", p.semT, st._lnsq_t[c - 2])
        p.op("scalar", lambda s, c=c: s.activation(xsq[:, c % 2, :], x_sb[:, c, :], AF.Square), inc=True)
        ssq = p.semS.n
        p.wait("tensor", p.semS, ssq)
        if c == 0:
            for nt in range(NT):
                p.claim_bank(nt)
        for nt in range(NT):
            p.op("tensor", lambda t, c=c, nt=nt: t.matmul(
                psum[nt][0:1, :], st.ones_col[:], xsq[:, c % 2, nt * 512:(nt + 1) * 512],
                start=(c == 0), stop=(c == NCH - 1)),
                inc=(nt == NT - 1))
        st._lnsq_t[c] = p.semT.n
    tmm2 = p.semT.n
    p.wait("vector", p.semT, tmm2)
    for nt in range(NT):
        p.op("vector", lambda v, nt=nt: v.tensor_copy(vrow[0:1, nt * 512:(nt + 1) * 512], psum[nt][0:1, :]),
             inc_sem=p.bankV[nt])
        p.evac_done(nt)

    if SUB < 3:
        p.barrier(); st.pop_region(); return
    p.op("vector", lambda v: v.tensor_scalar(mrow[:], mrow[:], 1.0 / D, None, OP.mult))
    p.op("vector", lambda v: v.tensor_scalar(vrow[:], vrow[:], 1.0 / D, None, OP.mult))
    p.op("vector", lambda v: v.tensor_tensor(xsq[0:1, 0, 0:T], mrow[:], mrow[:], OP.mult))
    vv = p.op("vector", lambda v: v.tensor_tensor(vrow[:], vrow[:], xsq[0:1, 0, 0:T], OP.subtract), inc=True)
    p.wait("scalar", p.semV, vv)
    sv = p.op("scalar", lambda s: s.activation(rsrow[:], vrow[:], AF.Sqrt, bias=st.eps_col[0:1, 0:1]), inc=True)
    p.wait("vector", p.semS, sv)
    vv = p.op("vector", lambda v: v.reciprocal(rsrow[:], rsrow[:]), inc=True)

    if SUB < 4:
        p.barrier(); st.pop_region(); return
    p.wait("sync", p.semV, vv)
    bcast_row(st, mrow[:], m_bc[:], 1)
    bcast_row(st, rsrow[:], rs_bc[:], 2)
    p.fence("vector", p.gsem("sBc"))

    dbg(st, "mrow", lambda: mrow[:], [1, T], F32)
    dbg(st, "vrow", lambda: vrow[:], [1, T], F32)
    dbg(st, "rsrow", lambda: rsrow[:], [1, T], F32)
    dbg(st, "mbc", lambda: m_bc[:], [128, T], F32)
    dbg(st, "rsbc", lambda: rs_bc[:], [128, T], F32)
    dbg(st, "x0sb", lambda: x_sb[:, 0, :], [128, T], F32)
    if SUB < 5:
        p.barrier(); st.pop_region(); return
    p.fence("vector", sWx)
    for c in range(NCH):
        p.op("vector", lambda v, c=c: v.tensor_tensor(x_sb[:, c, :], x_sb[:, c, :], m_bc[:], OP.subtract))
        p.op("vector", lambda v, c=c: v.tensor_tensor(xn[:, c, 1:T + 1], x_sb[:, c, :], rs_bc[:], OP.mult),
             inc=(c == NCH - 1))
    st.xn_done = p.semV.n

    p.barrier()
    st.pop_region()



def load_weight_bf16(st, name, sbuf_t, hbm_ap=None, queue="gpsimd"):
    p = st.p
    src = st.W[name] if hbm_ap is None else hbm_ap
    return p.dma(queue, lambda e: e.dma_start(out=sbuf_t[:], in_=src))


def fm_matmul(st, lhsT_fn, rhs_fn, out_m_chunks, k_chunks, evac_fn,
              banks=(0, 1, 2, 3), n_tiles=NT, n_size=512, wait_t=None):
    p = st.p
    bi = 0
    for mc in range(out_m_chunks):
        for nt in range(n_tiles):
            bank = banks[bi % len(banks)]
            bi += 1
            p.claim_bank(bank)
            if wait_t is not None:
                wait_t()
            for kc in range(k_chunks):
                p.op("tensor", lambda t, kc=kc, mc=mc, nt=nt, bank=bank: t.matmul(
                    st.psum[bank][0:_mm_m(lhsT_fn, kc, mc), 0:n_size],
                    lhsT_fn(kc, mc), rhs_fn(kc, nt),
                    start=(kc == 0), stop=(kc == k_chunks - 1)),
                    inc=(kc == k_chunks - 1))
            evac_fn(mc, nt, bank, p.semT.n)


def _mm_m(lhsT_fn, kc, mc):
    return lhsT_fn(kc, mc).shape[-1]



TH = 1024
NTH = TH // 512
NCK = TH // 128


def ph_l0_attn(st):
    nc, p, sb = st.nc, st.p, st.sb
    W = st.W

    st.push_region()

    C = {}
    C["wx0"] = sb("wx0", [128, NCH, 160], BF16)
    C["wx2"] = sb("wx2", [32, 5, D], BF16)
    C["xbias"] = sb("xbias", [128, 5 * NCH])
    C["mu0c"] = sb("mu0c", [128, NCH])
    C["wr"] = sb("wr", [128, NCH, 512], BF16)
    C["ww1"] = sb("ww1", [128, NCH, 64], BF16)
    C["ww2"] = sb("ww2", [64, 512], BF16)
    C["bw2"] = sb("bw2", [128, 4])
    C["wk"] = sb("wk", [128, NCH, 512], BF16)
    C["wv"] = sb("wv", [128, NCH, D], BF16)
    C["ucol"] = sb("ucol", [128, 4])

    lw = {}
    sWt = p.gsem("sWt")
    lw["wx0"] = p.dma("gpsimd", lambda e: e.dma_start(out=C["wx0"][:], in_=W["W_x0"].rearrange("(c p) m -> p c m", p=128)))
    lw["wx2"] = p.dma("gpsimd", lambda e: e.dma_start(out=C["wx2"][:], in_=W["W_x2"].rearrange("i k m -> k i m")), sem=sWt)
    lw["xbias"] = p.dma("gpsimd", lambda e: e.dma_start(out=C["xbias"][:], in_=W["x_bias"]), sem=sWt)
    lw["mu0c"] = p.dma("gpsimd", lambda e: e.dma_start(out=C["mu0c"][:], in_=W["mu0"]), sem=sWt)
    lw["wr"] = p.dma("gpsimd", lambda e: e.dma_start(out=C["wr"][:], in_=W["W_r"].rearrange("(c p) m -> p c m", p=128)))
    lw["ww1"] = p.dma("gpsimd", lambda e: e.dma_start(out=C["ww1"][:], in_=W["W_w1"].rearrange("(c p) m -> p c m", p=128)))
    lw["ww2"] = p.dma("gpsimd", lambda e: e.dma_start(out=C["ww2"][:], in_=W["W_w2"]), sem=sWt)
    lw["bw2"] = p.dma("gpsimd", lambda e: e.dma_start(out=C["bw2"][:], in_=W["b_w2"]), sem=sWt)
    lw["wk"] = p.dma("gpsimd", lambda e: e.dma_start(out=C["wk"][:], in_=W["W_k"].rearrange("(c p) m -> p c m", p=128)))
    lw["wv"] = p.dma("gpsimd", lambda e: e.dma_start(out=C["wv"][:], in_=W["W_v"].rearrange("(c p) m -> p c m", p=128)))
    lw["ucol"] = p.dma("gpsimd", lambda e: e.dma_start(out=C["ucol"][:], in_=W["bonus"]), sem=sWt)
    C["lw"] = lw

    C["S_f"] = sb("S_f", [128, 4, 256])
    C["S_bf"] = sb("S_bf", [128, 4, 256], BF16)
    C["Lcarry"] = sb("Lcarry", [128, 4])
    p.mset("vector", lambda v: v.memset(C["S_f"][:], 0.0))
    p.mset("vector", lambda v: v.memset(C["S_bf"][:], 0.0))
    p.mset("vector", lambda v: v.memset(C["Lcarry"][:], 0.0))
    st.sbf_v = p.semV.n

    for half in range(2):
        _l0_half(st, half, C)

    p.barrier()
    st.pop_region()


def _l0_half(st, half, C):
    nc, p, sb = st.nc, st.p, st.sb
    psum = st.psum
    xn = st.xn
    X0 = half * TH + 1

    st.push_region()
    mixed = sb("mixed", [128, NCH, TH], BF16)
    xp = sb("xp", [32, 5, TH], BF16)
    mu_t = sb("mu_t", [128, 2, 512], BF16)
    o_tm = sb("o_tm", [128, NCK, D], BF16)
    st.push_region()
    r_bf = sb("r_bf", [128, 4, TH], BF16)
    k_bf = sb("k_bf", [128, 4, TH], BF16)
    Lp = sb("Lp", [128, 4, TH + 1])
    v_tm = sb("v_tm", [128, NCK, D], BF16)
    e_sb = sb("e_sb", [128, 512])

    p.wait("vector", p.semV, st.xn_done)

    def emit_mix_mu0():
        for c in range(NCH):
            for nt in range(NTH):
                a = X0 + nt * 512
                o0 = nt * 512
                p.op("vector", lambda v, c=c, a=a, o0=o0: v.tensor_tensor(
                    mixed[:, c, o0:o0 + 512], xn[:, c, a - 1:a + 511],
                    xn[:, c, a:a + 512], OP.subtract))
                p.op("vector", lambda v, c=c, a=a, o0=o0: v.scalar_tensor_tensor(
                    mixed[:, c, o0:o0 + 512], mixed[:, c, o0:o0 + 512],
                    C["mu0c"][:, c:c + 1], xn[:, c, a:a + 512], OP.mult, OP.add),
                    inc=(c == NCH - 1 and o0 == TH - 512))
        return p.semV.n

    p.fence("vector", p.gsem("sWt"))
    vmix = emit_mix_mu0()

    p.fence("tensor", p.gsem("sWt"))
    p.wait("tensor", p.semV, vmix)
    for nt in range(NTH):
        for part, (m0, msz) in enumerate([(0, 128), (128, 32)]):
            bank = 2 * nt + part
            p.claim_bank(bank)
            for kc in range(NCH):
                p.op("tensor", lambda t, nt=nt, kc=kc, m0=m0, msz=msz, bank=bank: t.matmul(
                    psum[bank][0:msz, 0:512],
                    C["wx0"][:, kc, m0:m0 + msz], mixed[:, kc, nt * 512:(nt + 1) * 512],
                    start=(kc == 0), stop=(kc == NCH - 1)), inc=(kc == NCH - 1))
            tv = p.semT.n
            p.wait("scalar", p.semT, tv)
            if m0 == 0:
                for i in range(4):
                    p.op("scalar", lambda s, i=i, nt=nt, bank=bank: s.activation(
                        xp[0:32, i, nt * 512:(nt + 1) * 512],
                        psum[bank][32 * i:32 * i + 32, 0:512], AF.Tanh),
                        inc_sem=(p.bankV[bank] if i == 3 else None), inc=(i == 3))
            else:
                p.op("scalar", lambda s, nt=nt, bank=bank: s.activation(
                    xp[0:32, 4, nt * 512:(nt + 1) * 512],
                    psum[bank][0:32, 0:512], AF.Tanh), inc_sem=p.bankV[bank], inc=True)
            p.evac_done(bank)
    sxp = p.semS.n

    p.fence("tensor", p.gsem("sWt"))
    p.wait("tensor", p.semS, sxp)

    def mix_for(i, twait):
        first = True
        for mc in range(NCH):
            for nt in range(NTH):
                bank = 4 + ((mc * NTH + nt) % 2)
                p.claim_bank(bank)
                p.op("tensor", lambda t, i=i, mc=mc, nt=nt, bank=bank: t.matmul(
                    psum[bank][0:128, 0:512], C["wx2"][:, i, mc * 128:(mc + 1) * 128],
                    xp[0:32, i, nt * 512:(nt + 1) * 512], start=True, stop=True), inc=True)
                tv = p.semT.n
                p.wait("scalar", p.semT, tv)
                p.op("scalar", lambda s, i=i, mc=mc, bank=bank: s.activation(
                    mu_t[:, bank - 4, :], psum[bank][:, 0:512], AF.Identity,
                    bias=C["xbias"][:, i * NCH + mc:i * NCH + mc + 1]), inc=True)
                sv = p.semS.n
                p.wait("vector", p.semS, sv)
                if first and twait:
                    p.wait("vector", p.semT, twait)
                    first = False
                a = X0 + nt * 512
                o0 = nt * 512
                p.op("vector", lambda v, mc=mc, a=a, o0=o0: v.tensor_tensor(
                    mixed[:, mc, o0:o0 + 512], xn[:, mc, a - 1:a + 511],
                    xn[:, mc, a:a + 512], OP.subtract))
                p.op("vector", lambda v, mc=mc, o0=o0, bank=bank: v.tensor_tensor(
                    mixed[:, mc, o0:o0 + 512], mixed[:, mc, o0:o0 + 512], mu_t[:, bank - 4, :], OP.mult),
                    inc_sem=p.bankV[bank])
                p.op("vector", lambda v, mc=mc, a=a, o0=o0: v.tensor_tensor(
                    mixed[:, mc, o0:o0 + 512], mixed[:, mc, o0:o0 + 512],
                    xn[:, mc, a:a + 512], OP.add),
                    inc=(mc == NCH - 1 and o0 == TH - 512))
                p.evac_done(bank)
        return p.semV.n

    def evac_to(dst):
        def f(mc, nt, bank, tv):
            p.wait("vector", p.semT, tv)
            p.op("vector", lambda v, mc=mc, nt=nt, bank=bank: v.tensor_copy(
                dst[:, mc, nt * 512:(nt + 1) * 512], psum[bank][:, 0:512]),
                inc_sem=p.bankV[bank], inc=True)
            p.evac_done(bank)
        return f

    def proj_fm(w_fn, dst, m_chunks, k_chunks=NCH, evac=None):
        evac = evac or evac_to(dst)
        for mc in range(m_chunks):
            for nt in range(NTH):
                bank = (mc * NTH + nt) % 4
                p.claim_bank(bank)
                for kc in range(k_chunks):
                    lhsT = w_fn(kc, mc)
                    p.op("tensor", lambda t, kc=kc, mc=mc, nt=nt, bank=bank, lhsT=lhsT: t.matmul(
                        psum[bank][0:lhsT.shape[-1], 0:512],
                        lhsT, mixed[:, kc, nt * 512:(nt + 1) * 512],
                        start=(kc == 0), stop=(kc == k_chunks - 1)), inc=(kc == k_chunks - 1))
                evac(mc, nt, bank, p.semT.n)

    def proj_tm(w_t, dst_tm):
        for m in range(NCK):
            for hf in range(2):
                bank = 4 + ((m * 2 + hf) % 2)
                p.claim_bank(bank)
                for kc in range(NCH):
                    p.op("tensor", lambda t, m=m, hf=hf, kc=kc, bank=bank: t.matmul(
                        psum[bank][:, 0:512], mixed[:, kc, m * 128:(m + 1) * 128],
                        w_t[:, kc, hf * 512:(hf + 1) * 512],
                        start=(kc == 0), stop=(kc == NCH - 1)), inc=(kc == NCH - 1))
                tv = p.semT.n
                p.wait("vector", p.semT, tv)
                p.op("vector", lambda v, m=m, hf=hf, bank=bank: v.tensor_copy(
                    dst_tm[:, m, hf * 512:(hf + 1) * 512], psum[bank][:, 0:512]),
                    inc_sem=p.bankV[bank], inc=True)
                p.evac_done(bank)

    vw = mix_for(0, p.semT.n)
    p.fence("tensor", p.gsem("sWt"))
    p.wait("tensor", p.semV, vw)
    proj_fm(lambda kc, mc: C["wr"][:, kc, mc * 128:(mc + 1) * 128], r_bf, 4)

    vw = mix_for(1, p.semT.n)
    p.fence("tensor", p.gsem("sWt"))
    p.wait("tensor", p.semV, vw)
    w1h = sb("w1h", [64, TH], BF16)

    def evac_w1h(mc, nt, bank, tv):
        p.wait("scalar", p.semT, tv)
        p.op("scalar", lambda s, nt=nt, bank=bank: s.activation(
            w1h[0:64, nt * 512:(nt + 1) * 512], psum[bank][0:64, 0:512], AF.Tanh),
            inc_sem=p.bankV[bank], inc=True)
        p.evac_done(bank)

    proj_fm(lambda kc, mc: C["ww1"][:, kc, 0:64], None, 1, evac=evac_w1h)
    sw = p.semS.n
    p.fence("tensor", p.gsem("sWt"))
    p.fence("tensor", p.gsem("sWt"))
    p.wait("tensor", p.semS, sw)
    sLc = p.gsem("sLc")
    vv0 = p.op("vector", lambda v: v.tensor_copy(p.tinyV[:], p.tinyV[:]), inc=True)
    p.wait("sync", p.semV, vv0)
    dlc = p.dma("sync", lambda e: e.dma_start(out=Lp[:, :, 0:1], in_=C["Lcarry"][:, :, None], allow_slow_non_contiguous=True), sem=sLc)
    p.fence("vector", sLc)
    p.fence("tensor", sLc)
    for h in range(4):
        for nt in range(NTH):
            bank = nt % 2
            p.claim_bank(bank)
            p.op("tensor", lambda t, h=h, nt=nt, bank=bank: t.matmul(
                psum[bank][0:128, 0:512], C["ww2"][0:64, h * 128:(h + 1) * 128],
                w1h[0:64, nt * 512:(nt + 1) * 512], start=True, stop=True), inc=True)
            tv = p.semT.n
            p.wait("scalar", p.semT, tv)
            if nt or h:
                p.wait("scalar", p.semV, p.semV.n)
            p.op("scalar", lambda s, h=h, bank=bank: s.activation(
                e_sb[:], psum[bank][:, 0:512], AF.Exp,
                bias=C["bw2"][:, h:h + 1]), inc_sem=p.bankV[bank], inc=True)
            p.evac_done(bank)
            sv = p.semS.n
            p.wait("vector", p.semS, sv)
            init = (C["Lcarry"][:, h:h + 1] if nt == 0
                    else Lp[:, h, nt * 512:nt * 512 + 1])
            p.op("vector", lambda v, h=h, nt=nt, init=init: v.tensor_tensor_scan(
                Lp[:, h, 1 + nt * 512:1 + (nt + 1) * 512], e_sb[:], e_sb[:], init,
                OP.add, OP.bypass), inc=True)
    vsc = p.semV.n
    p.wait("sync", p.semV, vsc)
    p.dma("sync", lambda e: e.dma_start(out=C["Lcarry"][:, :, None], in_=Lp[:, :, TH:TH + 1], allow_slow_non_contiguous=True), sem=p.gsem("sLc"))
    p.fence("vector", p.gsem("sLc"))
    vLp = p.semV.n

    vw = mix_for(2, p.semT.n)
    p.fence("tensor", p.gsem("sWt"))
    p.wait("tensor", p.semV, vw)
    proj_fm(lambda kc, mc: C["wk"][:, kc, mc * 128:(mc + 1) * 128], k_bf, 4)

    vw = mix_for(3, p.semT.n)
    p.fence("tensor", p.gsem("sWt"))
    p.wait("tensor", p.semV, vw)
    proj_tm(C["wv"], v_tm)
    v_ready_v = p.semV.n

    if half == 0:
        dbg(st, "r0", lambda: r_bf[:], [128, 4, TH], BF16)
        dbg(st, "k0", lambda: k_bf[:], [128, 4, TH], BF16)
        dbg(st, "Lp0", lambda: Lp[:], [128, 4, TH + 1], F32)
        dbg(st, "v0", lambda: v_tm[:], [128, NCK, D], BF16)
        dbg(st, "xp0", lambda: xp[:], [32, 5, TH], BF16)

    _l0_scan(st, C, r_bf, k_bf, Lp, v_tm, o_tm, vLp, v_ready_v)
    if half == 0:
        dbg(st, "o0", lambda: o_tm[:], [128, NCK, D], BF16)
    st.pop_region()

    g_tm = sb("g_tm", [128, NCK, D], BF16)
    wg = sb("wg", [128, NCH, D], BF16)
    gnw_bc = sb("gnw_bc", [128, D], BF16)
    gnb_bc = sb("gnb_bc", [128, D], BF16)
    sWt2 = p.gsem("sWt2")
    ldg = p.dma("gpsimd", lambda e: e.dma_start(out=wg[:], in_=st.W["W_g"].rearrange("(c p) m -> p c m", p=128)), sem=sWt2)
    p.dma("gpsimd", lambda e: e.dma_start(out=gnw_bc[:], in_=st.W["gn_w"].partition_broadcast(128)), sem=sWt2)
    ldgn = p.dma("gpsimd", lambda e: e.dma_start(out=gnb_bc[:], in_=st.W["gn_b"].partition_broadcast(128)), sem=sWt2)
    vw = mix_for(4, p.semT.n)
    p.fence("tensor", p.gsem("sWt2"))
    p.wait("tensor", p.semV, vw)
    proj_tm(wg, g_tm)
    wo = wg
    p.wait("gpsimd", p.semT, p.semT.n)
    ldo = p.dma("gpsimd", lambda e: e.dma_start(out=wo[:], in_=st.W["W_o"].rearrange("(c p) m -> p c m", p=128)), sem=sWt2)
    if half == 0:
        dbg(st, "g0", lambda: g_tm[:], [128, NCK, D], BF16)
    C2 = dict(C)
    C2.update(dict(wg=wg, wo=wo, gnw_bc=gnw_bc, gnb_bc=gnb_bc, ldo=ldo, ldgn=ldgn))
    _l0_post(st, C2, half, o_tm, g_tm)

    p.barrier()
    st.pop_region()


def _l0_scan(st, C, r_bf, k_bf, Lp, v_tm, o_tm, vLp, v_ready_v):
    nc, p, sb = st.nc, st.p, st.sb
    psum, psT = st.psum, st.psT
    S_f, S_bf, ucol = C["S_f"], C["S_bf"], C["ucol"]

    st.push_region()
    RQ0, RQA = 0, 128
    RKj = [224, 256, 320, 416]
    RKPP = 544
    RW = 673
    dT = sb("dT", [128, NCK, 128], BF16)
    dec_f = sb("dec_f", [128, 4])

    p.wait("vector", p.semV, vLp)

    st.push_region()
    e2 = sb("e2", [128, 512], BF16)
    d_pad = sb("d_pad", [128, TH], BF16)
    p.mset("vector", lambda v: v.memset(d_pad[:], 0.0))
    for h in range(4):
        for nt in range(NTH):
            a = nt * 512
            p.op("vector", lambda v, h=h, a=a: v.tensor_tensor(
                e2[:], r_bf[:, h, a:a + 512], k_bf[:, h, a:a + 512], OP.mult))
            vv = p.op("vector", lambda v, h=h: v.tensor_scalar(
                e2[:], e2[:], ucol[:, h:h + 1], None, OP.mult), inc=True)
            p.wait("tensor", p.semV, vv)
            p.claim_bank(6)
            p.op("tensor", lambda t: t.matmul(
                psum[6][0:1, 0:512], st.ones_col_bf[:], e2[:],
                start=True, stop=True), inc=True)
            tv = p.semT.n
            p.wait("vector", p.semT, tv)
            p.op("vector", lambda v, h=h, a=a: v.tensor_copy(
                d_pad[32 * h:32 * h + 1, a:a + 512], psum[6][0:1, 0:512]),
                inc_sem=p.bankV[6], inc=True)
            p.evac_done(6)
    vdp = p.semV.n
    p.wait("sync", p.semV, vdp)
    sDt = p.gsem("sDt")
    for ci in range(NCK):
        p.dma("sync", lambda e, ci=ci: e.dma_start_transpose(
            dT[:, ci, :], d_pad[:, ci * 128:(ci + 1) * 128]), sem=sDt)
    p.fence("vector", p.gsem("sDt"))
    st.pop_region()
    st.push_region()
    rexp = sb("rexp", [128, 1, RW])
    qk = sb("qk", [128, 4, RW], BF16)
    A_sb = sb("A_sb", [128, 4, 128], BF16)
    kT = sb("kT", [128, 4, 128], BF16)

    PA, PO0, PO1, PS0, PS1 = 0, 1, 2, 3, 4
    tS_prev = 0
    se_prev = 0
    for ci in range(NCK):
        t0 = ci * 128
        for hp in range(4):
            if se_prev:
                p.wait("vector", p.semS, se_prev)
            for h in (hp,):
                hh = 0
                def B(j, h=h, t0=t0):
                    return Lp[:, h, t0 + 32 * j:t0 + 32 * j + 1]
                p.op("vector", lambda v, h=h, hh=hh, t0=t0, B=B: v.tensor_scalar(
                    rexp[:, hh, RQ0:RQ0 + 128], Lp[:, h, t0:t0 + 128], B(0), -1.0,
                    OP.subtract, OP.mult))
                for j in range(1, 4):
                    p.op("vector", lambda v, h=h, hh=hh, t0=t0, j=j, B=B: v.tensor_scalar(
                        rexp[:, hh, RQA + 32 * (j - 1):RQA + 32 * j],
                        Lp[:, h, t0 + 32 * j:t0 + 32 * (j + 1)], B(j), -1.0,
                        OP.subtract, OP.mult))
                for j in range(4):
                    p.op("vector", lambda v, h=h, hh=hh, t0=t0, j=j, B=B: v.tensor_scalar(
                        rexp[:, hh, RKj[j]:RKj[j] + 32 * (j + 1)],
                        Lp[:, h, t0 + 1:t0 + 1 + 32 * (j + 1)], B(j), None, OP.subtract))
                p.op("vector", lambda v, h=h, hh=hh, t0=t0: v.tensor_scalar(
                    rexp[:, hh, RKPP:RKPP + 129], Lp[:, h, t0:t0 + 129],
                    Lp[:, h, t0 + 128:t0 + 129], None, OP.subtract),
                    inc=True)
            vre = p.semV.n
            p.wait("scalar", p.semV, vre)
            se_prev = p.op("scalar", lambda s, hp=hp: s.activation(
                qk[:, hp:hp + 1, :], rexp[:], AF.Exp), inc=True)
        p.wait("vector", p.semS, se_prev)
        if tS_prev:
            p.wait("vector", p.semT, tS_prev)
        for h in range(4):
            p.op("vector", lambda v, h=h, t0=t0: v.scalar_tensor_tensor(
                qk[:, h, RQ0:RQ0 + 128], qk[:, h, RQ0:RQ0 + 128], 1.0,
                r_bf[:, h, t0:t0 + 128], OP.bypass, OP.mult))
            p.op("vector", lambda v, h=h, t0=t0: v.scalar_tensor_tensor(
                qk[:, h, RQA:RQA + 96], qk[:, h, RQA:RQA + 96], 1.0,
                r_bf[:, h, t0 + 32:t0 + 128], OP.bypass, OP.mult))
            for j in range(4):
                p.op("vector", lambda v, h=h, t0=t0, j=j: v.scalar_tensor_tensor(
                    qk[:, h, RKj[j]:RKj[j] + 32 * (j + 1)],
                    qk[:, h, RKj[j]:RKj[j] + 32 * (j + 1)], 1.0,
                    k_bf[:, h, t0:t0 + 32 * (j + 1)], OP.bypass, OP.mult))
            p.op("vector", lambda v, h=h, t0=t0: v.scalar_tensor_tensor(
                qk[:, h, RKPP + 1:RKPP + 129], qk[:, h, RKPP + 1:RKPP + 129], 1.0,
                k_bf[:, h, t0:t0 + 128], OP.bypass, OP.mult), inc=(h == 3))
        vqk = p.semV.n
        p.wait("tensor", p.semV, vqk)
        if ci == 0:
            p.wait("tensor", p.semV, v_ready_v)
            p.fence("tensor", p.gsem("sDt"))

        p.claim_bank(PA)
        for h in range(4):
            for j in range(4):
                rhs = (qk[:, h, RQ0 + 32 * 0:RQ0 + 32] if j == 0
                       else qk[:, h, RQA + 32 * (j - 1):RQA + 32 * j])
                if j == 0:
                    rhs = qk[:, h, RQ0:RQ0 + 32]
                p.op("tensor", lambda t, h=h, j=j, rhs=rhs: t.matmul(
                    psum[PA][0:32 * (j + 1), h * 128 + 32 * j:h * 128 + 32 * (j + 1)],
                    qk[:, h, RKj[j]:RKj[j] + 32 * (j + 1)], rhs,
                    start=True, stop=True), inc=(h == 3 and j == 3))
        tA = p.semT.n
        p.wait("vector", p.semT, tA)
        p.op("vector", lambda v: v.tensor_tensor(
            A_sb[:], psum[PA][:, 0:512].rearrange("p (h t) -> p h t", h=4),
            st.mask_sl[:, None, :].broadcast_to([128, 4, 128]), OP.mult),
            inc_sem=p.bankV[PA], inc=True)
        p.evac_done(PA)
        vA = p.semV.n

        p.claim_bank(7)
        for h in range(4):
            p.op("tensor", lambda t, h=h: t.transpose(
                psT[:, h * 128:(h + 1) * 128], qk[:, h, RKPP + 1:RKPP + 129], st.idt_bf[:]),
                inc=(h == 3))
        tKT = p.semT.n
        p.wait("vector", p.semT, tKT)
        p.op("vector", lambda v: v.tensor_copy(
            kT[:].rearrange("p h t -> p (h t)"), psT[:, 0:512]),
            inc_sem=p.bankV[7], inc=True)
        p.evac_done(7)
        vKT = p.semV.n

        p.wait("tensor", p.semV, vA)
        p.wait("tensor", p.semV, st.sbf_v)
        for h in range(4):
            bank = PO0 if h < 2 else PO1
            if h % 2 == 0:
                p.claim_bank(bank)
            p.op("tensor", lambda t, h=h, bank=bank, ci=ci: t.matmul(
                psum[bank][:, (h % 2) * 256:(h % 2) * 256 + 256],
                A_sb[:, h, :], v_tm[:, ci, h * 256:(h + 1) * 256],
                start=True, stop=False))
            p.op("tensor", lambda t, h=h, bank=bank: t.matmul(
                psum[bank][:, (h % 2) * 256:(h % 2) * 256 + 256],
                qk[:, h, RQ0:RQ0 + 128], S_bf[:, h, :],
                start=False, stop=True), inc=(h == 3))
        tO = p.semT.n

        p.wait("tensor", p.semV, vKT)
        for h in range(4):
            bank = PS0 if h < 2 else PS1
            if h % 2 == 0:
                p.claim_bank(bank)
            p.op("tensor", lambda t, h=h, bank=bank, ci=ci: t.matmul(
                psum[bank][:, (h % 2) * 256:(h % 2) * 256 + 256],
                kT[:, h, :], v_tm[:, ci, h * 256:(h + 1) * 256],
                start=True, stop=True), inc=(h == 3))
        tS_prev = p.semT.n

        p.wait("vector", p.semT, tO)
        for h in range(4):
            bank = PO0 if h < 2 else PO1
            p.op("vector", lambda v, h=h, bank=bank, ci=ci: v.scalar_tensor_tensor(
                o_tm[:, ci, h * 256:(h + 1) * 256], v_tm[:, ci, h * 256:(h + 1) * 256],
                dT[:, ci, 32 * h:32 * h + 1], psum[bank][:, (h % 2) * 256:(h % 2) * 256 + 256],
                OP.mult, OP.add), inc_sem=(p.bankV[bank] if h % 2 == 1 else None))
            if h % 2 == 1:
                p.evac_done(bank)
        p.wait("scalar", p.semV, vqk)
        sdec = p.op("scalar", lambda ss: ss.activation(
            dec_f[:], qk[:, :, RKPP:RKPP + 1].rearrange("p h o -> p (h o)"), AF.Identity), inc=True)
        p.wait("vector", p.semT, tS_prev)
        p.wait("vector", p.semS, sdec)
        for h in range(4):
            bank = PS0 if h < 2 else PS1
            p.op("vector", lambda v, h=h: v.tensor_scalar(
                S_f[:, h, :], S_f[:, h, :], dec_f[:, h:h + 1], None, OP.mult))
            p.op("vector", lambda v, h=h, bank=bank: v.tensor_tensor(
                S_f[:, h, :], S_f[:, h, :], psum[bank][:, (h % 2) * 256:(h % 2) * 256 + 256],
                OP.add), inc_sem=(p.bankV[bank] if h % 2 == 1 else None))
            if h % 2 == 1:
                p.evac_done(bank)
            p.op("vector", lambda v, h=h: v.tensor_copy(S_bf[:, h, :], S_f[:, h, :]),
                 inc=(h == 3))
        st.sbf_v = p.semV.n

    p.barrier()
    st.pop_region()
    st.pop_region()


def _l0_post(st, C, half, o_tm, g_tm):
    from kernel import dbg
    nc, p, sb = st.nc, st.p, st.sb
    psum, psT = st.psum, st.psT

    st.push_region()
    stat = sb("stat", [128, 24])
    osq = sb("osq", [128, 4, 256])
    silu_g = g_tm
    prod_fm = sb("prod_fm", [128, NCH, TH], BF16)
    aout = [sb(f"aout{i}", [128, 512]) for i in range(2)]
    st.aout_sems = [p.gsem("aouts0"), p.gsem("aouts1")]

    p.fence("vector", p.gsem("sWt2"))
    p.wait("scalar", p.semV, p.semV.n)
    for m in range(NCK):
        p.op("scalar", lambda s, m=m: s.activation(
            silu_g[:, m, :], g_tm[:, m, :], AF.Silu), inc=(m == NCK - 1))
    ssg = p.semS.n

    sums1 = sb("sums1", [128, 64])
    sums2 = sb("sums2", [128, 64])
    mw = sb("mw", [128, 64])
    rsw = sb("rsw", [128, 64])
    p.mset("vector", lambda v: v.memset(sums1[:], 1.0))
    p.mset("vector", lambda v: v.memset(sums2[:], 1.0))

    sq_prev = 0
    for m in range(NCK):
        if sq_prev:
            p.wait("scalar", p.semV, sq_prev)
        for h in range(4):
            p.op("scalar", lambda s, m=m, h=h: s.activation(
                osq[:, h, :], o_tm[:, m, h * 256:(h + 1) * 256], AF.Square), inc=(h == 3))
        sq = p.semS.n
        for h in range(4):
            p.op("vector", lambda v, m=m, h=h: v.tensor_reduce(
                sums1[:, m * 4 + h:m * 4 + h + 1], o_tm[:, m, h * 256:(h + 1) * 256], AX, OP.add))
        p.wait("vector", p.semS, sq)
        for h in range(4):
            p.op("vector", lambda v, m=m, h=h: v.tensor_reduce(
                sums2[:, m * 4 + h:m * 4 + h + 1], osq[:, h, :], AX, OP.add),
                inc=(h == 3))
        sq_prev = p.semV.n

    p.op("vector", lambda v: v.tensor_scalar(mw[:], sums1[:], 1.0 / 256, None, OP.mult))
    p.op("vector", lambda v: v.tensor_scalar(sums2[:], sums2[:], 1.0 / 256, None, OP.mult))
    p.op("vector", lambda v: v.scalar_tensor_tensor(sums1[:], mw[:], 1.0, mw[:], OP.bypass, OP.mult))
    vv = p.op("vector", lambda v: v.scalar_tensor_tensor(sums2[:], sums2[:], 1.0, sums1[:], OP.bypass, OP.subtract), inc=True)
    p.wait("scalar", p.semV, vv)
    sq = p.op("scalar", lambda s: s.activation(sums2[:], sums2[:], AF.Sqrt, bias=st.eps_col[:]), inc=True)
    p.wait("vector", p.semS, sq)
    vv = p.op("vector", lambda v: v.reciprocal(rsw[:], sums2[:]), inc=True)
    if half == 0:
        dbg(st, "stat0", lambda: rsw[:], [128, 64], F32)

    vprev_m = 0
    for m in range(NCK):
        for h in range(4):
            p.op("vector", lambda v, m=m, h=h: v.tensor_scalar(
                o_tm[:, m, h * 256:(h + 1) * 256], o_tm[:, m, h * 256:(h + 1) * 256],
                mw[:, m * 4 + h:m * 4 + h + 1], None, OP.subtract))
            p.op("vector", lambda v, m=m, h=h: v.tensor_scalar(
                o_tm[:, m, h * 256:(h + 1) * 256], o_tm[:, m, h * 256:(h + 1) * 256],
                rsw[:, m * 4 + h:m * 4 + h + 1], None, OP.mult))
        p.wait("vector", p.semS, ssg)
        p.op("vector", lambda v, m=m: v.tensor_tensor(
            o_tm[:, m, :], o_tm[:, m, :], C["gnw_bc"][:], OP.mult))
        p.op("vector", lambda v, m=m: v.tensor_tensor(
            o_tm[:, m, :], o_tm[:, m, :], C["gnb_bc"][:], OP.add))
        vv = p.op("vector", lambda v, m=m: v.tensor_tensor(
            o_tm[:, m, :], o_tm[:, m, :], silu_g[:, m, :], OP.mult), inc=True)
        if m == 0 and half == 0:
            dbg(st, "on0", lambda: o_tm[:, 0, :], [128, D], BF16)
            dbg(st, "sg0", lambda: silu_g[:, 0, :], [128, D], BF16)
        p.wait("tensor", p.semV, vv)
        p.claim_bank(7)
        for dc in range(NCH):
            p.op("tensor", lambda t, m=m, dc=dc: t.transpose(
                psT[:, dc * 128:(dc + 1) * 128], o_tm[:, m, dc * 128:(dc + 1) * 128],
                st.idt_bf[:]), inc=(dc == NCH - 1))
        tv = p.semT.n
        p.wait("vector", p.semT, tv)
        p.op("vector", lambda v, m=m: v.tensor_copy(
            prod_fm[:, :, m * 128:(m + 1) * 128],
            psT[:].rearrange("p (c t) -> p c t", c=NCH)), inc_sem=p.bankV[7], inc=True)
        p.evac_done(7)
        vprev_m = p.semV.n
    vprod = p.semV.n

    p.wait("tensor", p.semV, vprod)
    p.fence("tensor", p.gsem("sWt2"))
    xo = st.xout.rearrange("(c p) t -> p c t", p=128)
    for mc in range(NCH):
        for nt in range(NTH):
            bank = (mc * NTH + nt) % 4
            p.claim_bank(bank)
            for kc in range(NCH):
                p.op("tensor", lambda t, kc=kc, mc=mc, nt=nt, bank=bank: t.matmul(
                    psum[bank][:, 0:512], C["wo"][:, kc, mc * 128:(mc + 1) * 128],
                    prod_fm[:, kc, nt * 512:(nt + 1) * 512],
                    start=(kc == 0), stop=(kc == NCH - 1)), inc=(kc == NCH - 1))
            tv = p.semT.n
            ab = (mc * NTH + nt) % 2
            p.wait("vector", p.semT, tv)
            p.wait("vector", st.aout_sems[ab], st.aout_sems[ab].n)
            p.op("vector", lambda v, ab=ab, bank=bank: v.tensor_copy(
                aout[ab][:], psum[bank][:, 0:512]),
                inc_sem=p.bankV[bank], inc=True)
            p.evac_done(bank)
            va = p.semV.n
            p.wait("gpsimd", p.semV, va)
            p.dma("gpsimd", lambda e, mc=mc, nt=nt, ab=ab: e.dma_start(
                out=xo[:, mc, half * TH + nt * 512:half * TH + (nt + 1) * 512],
                in_=aout[ab][:], accum_op=OP.add), sem=st.aout_sems[ab])

    p.barrier()
    st.pop_region()



FTT = 512
NFT = T // FTT


def _cross_part_absmax(st, src, width, tmax_out):
    p = st.p
    nblk = width // 128
    sTx = p.gsem("sTx")
    p.wait("sync", p.semV, p.semV.n)
    for b in range(nblk):
        p.dma("sync", lambda e, b=b: e.dma_start_transpose(
            st.amaxT[:, b, :], src[:, b * 128:(b + 1) * 128]), sem=sTx)
    p.fence("vector", sTx)
    p.op("vector", lambda v, nblk=nblk: v.tensor_reduce(
        tmax_out[:, 0:nblk], st.amaxT[:, 0:nblk, :],
        AX, OP.max, apply_absolute_value=True), inc=True)
    return p.semV.n


def _bcast_tmax(st, tmax, width, row_idx, out_bc, sem):
    p = st.p
    nblk = width // 128
    p.dma("sync", lambda e: e.dma_start(
        out=st.rscr[row_idx:row_idx + 1, 0:width].rearrange("o (b r) -> o r b", r=128),
        in_=tmax[0:128, 0:nblk], allow_slow_non_contiguous=True), sem=sem)
    p.fence("sync", sem)
    p.dma("sync", lambda e: e.dma_start(
        out=out_bc[:, 0:width],
        in_=st.rscr[row_idx:row_idx + 1, 0:width].partition_broadcast(128)), sem=sem)


def _ternarize(st, wq, nchunks, wcols, mean_idx, sBc2):
    p = st.p
    R = st.tern
    p.op("vector", lambda v: v.tensor_reduce(
        R["absc"][:, 0:1], wq.rearrange("p c m -> p (c m)"),
        AX, OP.add, apply_absolute_value=True), inc=True)
    vr = p.semV.n
    p.wait("sync", p.semV, vr)
    p.dma("sync", lambda e: e.dma_start(
        out=st.rscr[3:4, 0:128].rearrange("o p -> p o"), in_=R["absc"][:, 0:1]), sem=sBc2)
    p.fence("sync", sBc2)
    p.dma("sync", lambda e: e.dma_start(out=R["row128"][:], in_=st.rscr[3:4, 0:128]), sem=sBc2)
    p.fence("vector", sBc2)
    p.op("vector", lambda v: v.tensor_reduce(R["rowT"][:], R["row128"][:], AX, OP.add), inc=True)
    vr2 = p.semV.n
    p.wait("sync", p.semV, vr2)
    p.dma("sync", lambda e: e.dma_start(out=st.rscr[5:6, 0:1], in_=R["rowT"][:]), sem=sBc2)
    p.fence("sync", sBc2)
    p.dma("sync", lambda e: e.dma_start(
        out=R["totc"][:], in_=st.rscr[5:6, 0:1].partition_broadcast(128)), sem=sBc2)
    p.fence("vector", sBc2)
    nelem = nchunks * 128 * wcols
    p.op("vector", lambda v, n=nelem: v.tensor_scalar(
        R["thrc"][:, 0:1], R["totc"][:], 0.5 / n, None, OP.mult))
    p.op("vector", lambda v, n=nelem: v.tensor_scalar(
        R["thrc"][:, 1:2], R["totc"][:], -0.5 / n, None, OP.mult))
    p.op("vector", lambda v, n=nelem, mi=mean_idx: v.tensor_scalar(
        st.meancols[:, mi:mi + 1], R["totc"][:], -1.0 / (127.0 * n), None, OP.mult), inc=True)
    mi = mean_idx
    dbg(st, f"tern{mi}_thrc", lambda: R["thrc"][:], [128, 2], F32)
    dbg(st, f"tern{mi}_mc", lambda: st.meancols[:], [128, 4], F32)
    for c in range(nchunks):
        for a in range(0, wcols, T):
            b = min(a + T, wcols)
            p.op("vector", lambda v, c=c, a=a, b=b: v.tensor_scalar(
                R["tscr"][:, 0:b - a], wq[:, c, a:b], R["thrc"][:, 0:1], None, OP.is_ge))
            p.op("vector", lambda v, c=c, a=a, b=b: v.scalar_tensor_tensor(
                wq[:, c, a:b], wq[:, c, a:b], R["thrc"][:, 1:2], R["tscr"][:, 0:b - a],
                OP.is_le, OP.subtract), inc=(c == nchunks - 1 and b == wcols))
    return p.semV.n


def _round_chain(st, dst, src_q, width, vq, pp):
    p = st.p
    n2, cb = st.n2[:, pp, 0:width], st.cbuf[:, pp, 0:width]
    p.wait("scalar", p.semV, vq)
    sv = p.op("scalar", lambda s_: s_.activation(
        n2, src_q, AF.Identity, bias=st.b128_col[:]), inc=True)
    p.wait("vector", p.semS, sv)
    p.op("vector", lambda v: v.scalar_tensor_tensor(
        cb, n2, -127.5, src_q, OP.add, OP.is_le))
    p.op("vector", lambda v: v.scalar_tensor_tensor(
        dst, n2, -128.0, cb, OP.add, OP.add), inc=True)
    return p.semV.n


def ph_l0_ffn(st):
    nc, p, sb = st.nc, st.p, st.sb
    psum = st.psum
    xn = st.xn
    W = st.W

    st.push_region()
    sBc2 = p.gsem("sBc2")
    hscr = nc.dram_tensor("hscr", [128, 32, T], BF16, kind="Internal").ap()
    st.amaxT = sb("amaxT", [128, T // 128, 128], BF16)
    st.meancols = sb("meancols", [128, 4])
    st.tern = dict(absc=sb("absc", [128, 1]), thrc=sb("thrc", [128, 2]),
                   row128=sb("row128", [1, 128]), rowT=sb("rowT", [1, 1]),
                   totc=sb("totc", [128, 1]), tscr=sb("tscr", [128, T], BF16))
    mxk = sb("mxk", [128, NFT, FTT])

    _l0_ffn_a(st, hscr, mxk, sBc2)
    _l0_ffn_b(st, hscr, mxk, sBc2)
    st.pop_region()


def _l0_ffn_a(st, hscr, mxk, sBc2):
    nc, p, sb = st.nc, st.p, st.sb
    psum = st.psum
    xn = st.xn
    W = st.W

    st.push_region()
    sFw = p.gsem("sFw")
    sHs = p.gsem("sHs")

    w1q = sb("w1q", [128, NCH, 4096], BF16)
    p.dma("gpsimd", lambda e: e.dma_start(
        out=w1q[:], in_=W["w1"].rearrange("(c p) m -> p c m", p=128)), sem=sFw)
    cwc = sb("cwc", [128, 32, 3])
    cbc = sb("cbc", [128, 32])
    p.dma("gpsimd", lambda e: e.dma_start(out=cwc[:], in_=W["conv_w"]), sem=sFw)
    p.dma("gpsimd", lambda e: e.dma_start(out=cbc[:], in_=W["conv_b"]), sem=sFw)

    amax = st.tern["tscr"]
    tmaxx = sb("tmaxx", [128, T // 128])
    sx_bc = sb("sx_bc", [128, T])
    m1_bc = sb("m1_bc", [128, T + 2])
    qbuf = sb("qbuf", [128, 2, FTT])
    st.n2 = sb("n2q", [128, 2, FTT], I32)
    st.cbuf = sb("cbufq", [128, 2, FTT], BF16)
    h1 = sb("h1", [128, 32, FTT + 2], BF16)
    acv = sb("acv", [128, 2, FTT])
    amx2 = sb("amx2", [128, FTT], BF16)
    tmax2 = sb("tmax2", [128, FTT // 128])
    sx2 = sb("sx2", [128, FTT])

    p.fence("vector", sFw)
    p.fence("scalar", sFw)
    vw1 = _ternarize(st, w1q, NCH, 4096, 0, sBc2)

    p.wait("vector", p.semV, st.xn_done)
    p.op("vector", lambda v: v.tensor_reduce(
        amax[:, 0:T], xn[:, :, 1:T + 1].rearrange("p c t -> p t c"),
        AX, OP.max, apply_absolute_value=True), inc=True)
    vx = _cross_part_absmax(st, amax, T, tmaxx)
    p.wait("sync", p.semV, vx)
    _bcast_tmax(st, tmaxx, T, 6, sx_bc, sBc2)
    p.fence("vector", sBc2)
    p.mset("vector", lambda v: v.memset(m1_bc[:, 0:1], 0.0))
    p.mset("vector", lambda v: v.memset(m1_bc[:, T + 1:T + 2], 0.0))
    p.op("vector", lambda v: v.tensor_scalar(sx_bc[:], sx_bc[:], 1e-5, None, OP.max))
    p.op("vector", lambda v: v.tensor_scalar(
        m1_bc[:, 1:T + 1], sx_bc[:], st.meancols[:, 0:1], None, OP.mult))
    p.op("vector", lambda v: v.reciprocal(sx_bc[:], sx_bc[:]))
    p.op("vector", lambda v: v.tensor_scalar(sx_bc[:], sx_bc[:], 127.0, None, OP.mult), inc=True)

    pp = 0
    for c in range(NCH):
        for q4 in range(T // FTT):
            a = 1 + q4 * FTT
            vq = p.op("vector", lambda v, c=c, a=a, q4=q4, pp=pp: v.tensor_tensor(
                qbuf[:, pp, :], xn[:, c, a:a + FTT], sx_bc[:, q4 * FTT:(q4 + 1) * FTT],
                OP.mult), inc=True)
            _round_chain(st, xn[:, c, a:a + FTT], qbuf[:, pp, :], FTT, vq, pp)
            pp ^= 1
    vxq = p.semV.n

    p.wait("tensor", p.semV, vxq)
    for tt in range(NFT):
        t0 = tt * FTT
        if tt > 0:
            p.wait("vector", sHs, sHs.n)
        vh1 = {}
        for mc in range(32):
            for g, (c0, cn) in enumerate([(0, 257), (257, 257)]):
                bank = (mc * 2 + g) % 4
                p.claim_bank(bank)
                for kc in range(NCH):
                    p.op("tensor", lambda t, mc=mc, kc=kc, t0=t0, c0=c0, cn=cn, bank=bank: t.matmul(
                        psum[bank][:, 0:cn], w1q[:, kc, mc * 128:(mc + 1) * 128],
                        xn[:, kc, t0 + c0:t0 + c0 + cn], start=(kc == 0), stop=(kc == NCH - 1)),
                        inc=(kc == NCH - 1))
                tv = p.semT.n
                p.wait("vector", p.semT, tv)
                p.op("vector", lambda v, mc=mc, bank=bank, t0=t0, c0=c0, cn=cn: v.tensor_tensor(
                    h1[:, mc, c0:c0 + cn], psum[bank][:, 0:cn],
                    m1_bc[:, t0 + c0:t0 + c0 + cn], OP.mult),
                    inc_sem=p.bankV[bank], inc=True)
                p.evac_done(bank)
            vh1[mc] = p.semV.n

        vstt = {}
        ssil = {}
        for mc in range(32):
            ab = mc % 2
            p.wait("scalar", p.semV, max(vh1[mc], vstt.get(mc - 2, 0)))
            st0 = p.op("scalar", lambda s_, mc=mc, ab=ab: s_.activation(
                acv[:, ab, :], h1[:, mc, 0:FTT], AF.Identity, scale=cwc[:, mc, 0:1]), inc=True)
            p.wait("vector", p.semS, st0)
            p.op("vector", lambda v, mc=mc, ab=ab: v.scalar_tensor_tensor(
                acv[:, ab, :], h1[:, mc, 1:FTT + 1], cwc[:, mc, 1:2], acv[:, ab, :],
                OP.mult, OP.add))
            vstt[mc] = p.op("vector", lambda v, mc=mc, ab=ab: v.scalar_tensor_tensor(
                acv[:, ab, :], h1[:, mc, 2:FTT + 2], cwc[:, mc, 2:3], acv[:, ab, :],
                OP.mult, OP.add), inc=True)
            p.wait("scalar", p.semV, vstt[mc])
            ssil[mc] = p.op("scalar", lambda s_, mc=mc, ab=ab: s_.activation(
                h1[:, mc, 0:FTT], acv[:, ab, :], AF.Silu, bias=cbc[:, mc:mc + 1]), inc=True)

        p.wait("vector", p.semS, ssil[31])
        p.op("vector", lambda v: v.tensor_reduce(
            amx2[:], h1[:, :, 0:FTT].rearrange("p c t -> p t c"),
            AX, OP.max, apply_absolute_value=True), inc=True)
        vm = _cross_part_absmax(st, amx2, FTT, tmax2)
        p.wait("sync", p.semV, vm)
        _bcast_tmax(st, tmax2, FTT, 7, sx2, sBc2)
        p.fence("vector", sBc2)
        p.op("vector", lambda v: v.tensor_scalar(sx2[:], sx2[:], 1e-5, None, OP.max))
        p.op("vector", lambda v, tt=tt: v.tensor_copy(mxk[:, tt, :], sx2[:]))
        p.op("vector", lambda v: v.reciprocal(sx2[:], sx2[:]))
        p.op("vector", lambda v: v.tensor_scalar(sx2[:], sx2[:], 127.0, None, OP.mult), inc=True)
        pp = 0
        for mc in range(32):
            vq = p.op("vector", lambda v, mc=mc, pp=pp: v.tensor_tensor(
                qbuf[:, pp, :], h1[:, mc, 0:FTT], sx2[:], OP.mult), inc=True)
            _round_chain(st, h1[:, mc, 0:FTT], qbuf[:, pp, :], FTT, vq, pp)
            pp ^= 1
        vq2 = p.semV.n
        if tt == 0:
            dbg(st, "h2q0", lambda: h1[:, :, 0:FTT], [128, 32, FTT], BF16)
        p.wait("gpsimd", p.semV, vq2)
        p.dma("gpsimd", lambda e, tt=tt: e.dma_start(
            out=hscr[:, :, tt * FTT:(tt + 1) * FTT], in_=h1[:, :, 0:FTT]), sem=sHs)

    dbg(st, "xq", lambda: xn[:, :, 1:T + 1], [128, NCH, T], BF16)
    dbg(st, "m1", lambda: m1_bc[:, 1:T + 1], [128, T], F32)
    p.barrier()
    st.pop_region()


def _l0_ffn_b(st, hscr, mxk, sBc2):
    nc, p, sb = st.nc, st.p, st.sb
    psum = st.psum
    W = st.W
    st.push_region()
    sFw2 = p.gsem("sFw2")
    sHl = [p.gsem("sHl0"), p.gsem("sHl1")]
    sFo = [p.gsem("sFo0"), p.gsem("sFo1")]

    w2q = sb("w2q", [128, 32, D], BF16)
    h2t = [sb(f"h2t{i}", [128, 32, FTT], BF16) for i in range(2)]
    fout = [sb(f"fout{i}", [128, FTT]) for i in range(2)]
    m2_bc = sb("m2_bc", [128, FTT])

    p.dma("gpsimd", lambda e: e.dma_start(
        out=w2q[:], in_=W["w2"].rearrange("(c p) m -> p c m", p=128)), sem=sFw2)
    p.dma("gpsimd", lambda e: e.dma_start(out=h2t[0][:], in_=hscr[:, :, 0:FTT]), sem=sHl[0])
    p.fence("vector", sFw2)
    vw2 = _ternarize(st, w2q, 32, D, 2, sBc2)

    xo = st.xout.rearrange("(c p) t -> p c t", p=128)
    t_end = {}
    for tt in range(NFT):
        ab = tt % 2
        if tt + 1 < NFT:
            if tt >= 1:
                p.wait("gpsimd", p.semT, t_end[tt - 1])
            p.dma("gpsimd", lambda e, tt=tt: e.dma_start(
                out=h2t[(tt + 1) % 2][:], in_=hscr[:, :, (tt + 1) * FTT:(tt + 2) * FTT]),
                sem=sHl[(tt + 1) % 2])
        p.op("vector", lambda v, tt=tt: v.tensor_scalar(
            m2_bc[:], mxk[:, tt, :], st.meancols[:, 2:3], None, OP.mult), inc=True)
        p.fence("tensor", sHl[ab])
        if tt == 0:
            p.wait("tensor", p.semV, vw2)
        for mc in range(NCH):
            bank = mc % 4
            p.claim_bank(bank)
            for kc in range(32):
                p.op("tensor", lambda t, mc=mc, kc=kc, ab=ab, bank=bank: t.matmul(
                    psum[bank][:, 0:FTT], w2q[:, kc, mc * 128:(mc + 1) * 128],
                    h2t[ab][:, kc, :], start=(kc == 0), stop=(kc == 31)),
                    inc=(kc == 31))
            tv = p.semT.n
            fb = mc % 2
            p.wait("vector", p.semT, tv)
            p.wait("vector", sFo[fb], sFo[fb].n)
            p.op("vector", lambda v, fb=fb, bank=bank: v.tensor_tensor(
                fout[fb][:], psum[bank][:, 0:FTT], m2_bc[:], OP.mult),
                inc_sem=p.bankV[bank], inc=True)
            p.evac_done(bank)
            vout = p.semV.n
            p.wait("gpsimd", p.semV, vout)
            p.dma("gpsimd", lambda e, tt=tt, mc=mc, fb=fb: e.dma_start(
                out=xo[:, mc, tt * FTT:(tt + 1) * FTT], in_=fout[fb][:],
                accum_op=OP.add), sem=sFo[fb])
        t_end[tt] = p.semT.n

    p.barrier()
    st.pop_region()



TPI = 0.15915494309189535
TWO_PI = 6.283185307179586
HALF_PI = 1.5707963267948966


def ph_l1_attn(st):
    nc, p, sb = st.nc, st.p, st.sb
    psum, psT = st.psum, st.psT
    xn = st.xn
    W = st.W

    st.push_region()
    sW3 = p.gsem("sW3")
    sPw = p.gsem("sPw")
    sBd = p.gsem("sBd")
    pwd = nc.dram_tensor("pwd", [32, 512], BF16, kind="Internal").ap()
    wdscr = nc.dram_tensor("wdscr", [2, 32, 32], BF16, kind="Internal").ap()

    w_in = sb("w_in", [128, NCH, 2080], BF16)
    ckc = sb("ckc", [128, 17, 4])
    theta = sb("thetac", [128, NCH])
    slopes = sb("slopes", [32, 1])
    ssc = sb("ssc", [32, 1])
    nsre = sb("nsre", [128, 1])
    nsim = sb("nsim", [128, 1])
    wre32 = sb("wre32", [32, 2, 32])
    wre_bf = sb("wre_bf", [32, 2, 32], BF16)
    wre_bd = sb("wre_bd", [128, 2, 128], BF16)
    w_out = sb("w_out", [128, NCH, D], BF16)
    lw = sb("lw", [32, T])
    pi2c = sb("pi2c", [128, 1])
    g32b = sb("g32b", [128, NCH, 32], BF16)
    G32 = [g32b[:, c, :] for c in range(NCH)]

    p.dma("gpsimd", lambda e: e.dma_start(
        out=w_in[:], in_=W["W_in"].rearrange("(c p) m -> p c m", p=128)), sem=sW3)
    p.dma("gpsimd", lambda e: e.dma_start(out=ckc[:], in_=W["ck"]), sem=sW3)
    p.dma("gpsimd", lambda e: e.dma_start(out=theta[:], in_=W["theta"]), sem=sW3)
    p.dma("gpsimd", lambda e: e.dma_start(out=slopes[:], in_=W["slopes"]), sem=sW3)
    p.dma("gpsimd", lambda e: e.dma_start(out=ssc[:], in_=W["score_scale"]), sem=sW3)
    p.dma("gpsimd", lambda e: e.dma_start(out=nsre[:], in_=W["ns_re"]), sem=sW3)
    p.dma("gpsimd", lambda e: e.dma_start(out=nsim[:], in_=W["ns_im"]), sem=sW3)
    p.dma("gpsimd", lambda e: e.dma_start(out=wre32[:, 0, :], in_=W["W_re"]), sem=sW3)
    p.dma("gpsimd", lambda e: e.dma_start(out=wre32[:, 1, :], in_=W["W_im"]), sem=sW3)
    p.dma("gpsimd", lambda e: e.dma_start(
        out=w_out[:], in_=W["W_out"].rearrange("(c p) m -> p c m", p=128)), sem=sW3)
    p.mset("vector", lambda v: v.memset(pi2c[:], HALF_PI))

    st.push_region()
    dvec = sb("dvec", [32, T])
    drow = sb("drow", [1, T])
    spn = sb("spn", [32, 1])
    g32f = sb("g32f", [128, NCH, 64])
    pbM = sb("pbM", [128, 64])
    g1s = sb("g1s", [128, 64])
    p.wait("vector", p.semV, st.const_v)
    vd = p.op("vector", lambda v: v.tensor_scalar(
        drow[:], st.iota_row[:], -1.0, float(T - 1), OP.mult, OP.add), inc=True)
    p.wait("sync", p.semV, vd)
    sLw = p.gsem("sLw")
    p.dma("sync", lambda e: e.dma_start(out=st.rscr[4:5, :], in_=drow[:]), sem=sLw)
    p.fence("sync", sLw)
    p.dma("sync", lambda e: e.dma_start(
        out=dvec[0:28, :], in_=st.rscr[4:5, :].partition_broadcast(28)), sem=sLw)
    p.dma("sync", lambda e: e.dma_start(
        out=dvec[28:32, :], in_=st.rscr[0:1, :].partition_broadcast(4)), sem=sLw)
    p.fence("scalar", sW3)
    se = p.op("scalar", lambda s_: s_.activation(spn[:], slopes[:], AF.Exp), inc=True)
    p.wait("vector", p.semS, se)
    ve = p.op("vector", lambda v: v.tensor_scalar(spn[:], spn[:], 1.0, None, OP.add), inc=True)
    p.wait("scalar", p.semV, ve)
    se = p.op("scalar", lambda s_: s_.activation(spn[:], spn[:], AF.Ln), inc=True)
    p.wait("vector", p.semS, se)
    ve = p.op("vector", lambda v: v.tensor_scalar(spn[:], spn[:], -1.0, None, OP.mult), inc=True)
    p.fence("vector", sLw)
    p.op("vector", lambda v: v.tensor_scalar(lw[:], dvec[:], spn[:, 0:1], None, OP.mult), inc=True)

    p.op("vector", lambda v: v.scalar_tensor_tensor(
        pbM[:], st.iota_bc[:, 0:64], -32.0, st.pcol[:, 0:1].broadcast_to([128, 64]),
        OP.mult, OP.add))
    for c in range(NCH):
        p.op("vector", lambda v, c=c: v.tensor_scalar(
            g1s[:], pbM[:], -128.0 * c - 0.5, None, OP.is_gt))
        p.op("vector", lambda v, c=c: v.scalar_tensor_tensor(
            g32f[:, c, :], pbM[:], -128.0 * c + 31.5, g1s[:], OP.is_le, OP.mult))
    p.op("vector", lambda v: v.tensor_copy(
        g32b[:], g32f[:, :, 0:32]), inc=True)
    dbg(st, "G321", lambda: g32b[:, 1, :], [128, 32], BF16)
    p.barrier()
    st.pop_region()

    p.fence("vector", sW3)
    p.op("vector", lambda v: v.tensor_scalar(wre32[:, 0, :], wre32[:, 0, :], nsre[0:32, 0:1], None, OP.mult))
    p.op("vector", lambda v: v.tensor_scalar(wre32[:, 1, :], wre32[:, 1, :], nsim[0:32, 0:1], None, OP.mult))
    vw = p.op("vector", lambda v: v.tensor_copy(wre_bf[:], wre32[:]), inc=True)
    p.mset("vector", lambda v: v.memset(wre_bd[:], 0.0))
    p.wait("sync", p.semV, vw)
    p.dma("sync", lambda e: e.dma_start(out=wdscr[0], in_=wre_bf[:, 0, :]), sem=sBd)
    p.dma("sync", lambda e: e.dma_start(out=wdscr[1], in_=wre_bf[:, 1, :]), sem=sBd)
    p.fence("sync", sBd)
    for ri in range(2):
        for j in range(4):
            p.dma("sync", lambda e, ri=ri, j=j: e.dma_start(
                out=wre_bd[32 * j:32 * (j + 1), ri, 32 * j:32 * (j + 1)], in_=wdscr[ri]), sem=sBd)
    p.fence("tensor", sBd)

    zv = sb("zv", [128, NCH, 515])
    zg = sb("zg", [128, NCH, 515], BF16)
    zs = sb("zs", [32, 515])
    gsave = sb("gsave", [128, NCH, 3], BF16)
    acv = sb("acv1", [128, 2, 512])
    kbb = sb("kbb", [128, NCH, 512], BF16)
    p_w = sb("p_w", [32, 512])
    phi = sb("phi", [128, 512])
    qs = sb("qs", [128, 512])
    sinb = sb("sinb", [128, 512])
    cosb = sb("cosb", [128, 512])
    cbx = sb("cbx", [128, 512], BF16)
    nsi = sb("nsi", [128, 1, 512], I32)
    ncs = [sb("recs", [128, NCH, 512]), sb("imcs", [128, NCH, 512])]
    nbf = [sb("rebf", [128, 512], BF16), sb("imbf", [128, 512], BF16)]
    carry = [sb("reC", [128, NCH]), sb("imC", [128, NCH])]
    denC = sb("denC", [32, 1])
    den = sb("den", [32, 512])
    s2 = sb("s2w", [32, 512])
    clw = sb("clw", [32, 512])
    cl2 = sb("cl2w", [32, 512])
    sqbf = sb("sqbf", [128, 512], BF16)
    prod1 = sb("prod1", [128, NCH, 512], BF16)
    sconv = clw
    aout = [acv[:, 0, :], acv[:, 1, :]]
    aosem = [p.gsem("aouts0"), p.gsem("aouts1")]

    p.mset("vector", lambda v: v.memset(zv[:, :, 0:3], 0.0))
    p.mset("vector", lambda v: v.memset(zg[:, :, 0:3], 0.0))
    p.mset("vector", lambda v: v.memset(zs[:, 0:3], 0.0))
    p.mset("vector", lambda v: v.memset(gsave[:], 0.0))
    p.mset("vector", lambda v: v.memset(carry[0][:], 0.0))
    p.mset("vector", lambda v: v.memset(carry[1][:], 0.0))
    p.mset("vector", lambda v: v.memset(denC[:], 0.0))

    for tt in range(4):
        _l1_tile(st, tt, locals())

    p.barrier()
    st.pop_region()


def _l1_tile(st, tt, L):
    nc, p, sb = st.nc, st.p, st.sb
    psum, psT = st.psum, st.psT
    xn = st.xn
    t0 = tt * 512
    zv, zg, zs, acv, kbb = L["zv"], L["zg"], L["zs"], L["acv"], L["kbb"]
    p_w, sconv, phi, qs = L["p_w"], L["sconv"], L["phi"], L["qs"]
    sinb, cosb, nsi, cbx = L["sinb"], L["cosb"], L["nsi"], L["cbx"]
    ncs, nbf, carry, denC, den = L["ncs"], L["nbf"], L["carry"], L["denC"], L["den"]
    s2, clw, cl2, sqbf = L["s2"], L["clw"], L["cl2"], L["sqbf"]
    prod1, aout, aosem = L["prod1"], L["aout"], L["aosem"]
    w_in, ckc, theta, ssc = L["w_in"], L["ckc"], L["theta"], L["ssc"]
    wre_bd, w_out, lw, pi2c, G32 = L["wre_bd"], L["w_out"], L["lw"], L["pi2c"], L["G32"]
    gsave = L["gsave"]
    sPw = p.gsem("sPw")

    if tt > 0:
        for eng in ("vector", "scalar"):
            p.wait(eng, L["aosem"][0], L["aosem"][0].n)
            p.wait(eng, L["aosem"][1], L["aosem"][1].n)
        p.op("vector", lambda v: v.tensor_copy(zv[:, :, 0:3], zv[:, :, 512:515]))
        p.op("vector", lambda v: v.tensor_copy(zg[:, :, 0:3], gsave[:]))
        p.op("vector", lambda v: v.tensor_copy(zs[:, 0:3], zs[:, 512:515]), inc=True)
    vhalo = p.semV.n
    p.wait("tensor", p.semV, vhalo)

    if tt == 0:
        p.wait("tensor", p.semV, st.xn_done)
        p.fence("tensor", p.gsem("sW3"))
    for zc in range(17):
        bank = zc % 4
        p.claim_bank(bank)
        m0, msz = zc * 128, (128 if zc < 16 else 32)
        for kc in range(NCH):
            p.op("tensor", lambda t, zc=zc, kc=kc, m0=m0, msz=msz, bank=bank: t.matmul(
                psum[bank][0:msz, 0:512], w_in[:, kc, m0:m0 + msz],
                xn[:, kc, 1 + t0:1 + t0 + 512], start=(kc == 0), stop=(kc == NCH - 1)),
                inc=(kc == NCH - 1))
        tv = p.semT.n
        p.wait("vector", p.semT, tv)
        if zc < 8:
            p.op("vector", lambda v, zc=zc, bank=bank: v.tensor_copy(
                zv[:, zc, 3:515], psum[bank][:, 0:512]), inc_sem=p.bankV[bank], inc=True)
        elif zc < 16:
            p.op("vector", lambda v, zc=zc, bank=bank: v.tensor_copy(
                zg[:, zc - 8, 3:515], psum[bank][:, 0:512]), inc_sem=p.bankV[bank], inc=True)
        else:
            p.op("vector", lambda v, bank=bank: v.tensor_copy(
                zs[0:32, 3:515], psum[bank][0:32, 0:512]), inc_sem=p.bankV[bank], inc=True)
        p.evac_done(bank)
    p.op("vector", lambda v: v.tensor_copy(gsave[:], zg[:, :, 512:515]), inc=True)
    vz = p.semV.n

    p.wait("scalar", p.semV, vz)
    s0 = p.op("scalar", lambda s_: s_.activation(
        sconv[0:32, :], zs[0:32, 0:512], AF.Identity, scale=ckc[0:32, 16, 0:1]), inc=True)
    p.wait("vector", p.semS, s0)
    for w in range(1, 4):
        p.op("vector", lambda v, w=w: v.scalar_tensor_tensor(
            sconv[0:32, :], zs[0:32, w:512 + w], ckc[0:32, 16, w:w + 1], sconv[0:32, :],
            OP.mult, OP.add))
    p.op("vector", lambda v: v.tensor_scalar(sconv[0:32, :], sconv[0:32, :], L["ssc"][0:32, 0:1], None, OP.mult))
    vpre = p.op("vector", lambda v, t0=t0: v.tensor_tensor(
        sconv[0:32, :], sconv[0:32, :], lw[0:32, t0:t0 + 512], OP.add), inc=True)
    p.wait("scalar", p.semV, vpre)
    spw = p.op("scalar", lambda s_: s_.activation(p_w[0:32, :], sconv[0:32, :], AF.Exp), inc=True)
    p.wait("vector", p.semS, spw)
    vden = p.op("vector", lambda v: v.tensor_tensor_scan(
        den[0:32, :], p_w[0:32, :], p_w[0:32, :], denC[0:32, 0:1], OP.add, OP.bypass), inc=True)
    p.wait("gpsimd", p.semS, spw)
    p.wait("gpsimd", p.semV, p.semV.n)
    p.dma("gpsimd", lambda e: e.dma_start(out=L["pwd"], in_=p_w[0:32, :]), sem=sPw)
    p.fence("gpsimd", sPw)
    for c in range(NCH):
        for j in range(4):
            p.dma("gpsimd", lambda e, c=c, j=j: e.dma_start(
                out=kbb[32 * j:32 * (j + 1), c, :],
                in_=L["pwd"][4 * c + j:4 * c + j + 1, :].partition_broadcast(32)), sem=sPw)
    p.fence("vector", sPw)

    sq_prev = 0
    for c in range(NCH):
        ab = c % 2
        if sq_prev:
            p.wait("scalar", p.semV, sq_prev)
        s0 = p.op("scalar", lambda s_, c=c, ab=ab: s_.activation(
            acv[:, ab, :], zv[:, c, 0:512], AF.Identity, scale=ckc[:, c, 0:1]), inc=True)
        p.wait("vector", p.semS, s0)
        for w in range(1, 4):
            p.op("vector", lambda v, c=c, w=w, ab=ab: v.scalar_tensor_tensor(
                acv[:, ab, :], zv[:, c, w:512 + w], ckc[:, c, w:w + 1], acv[:, ab, :],
                OP.mult, OP.add))
        p.op("vector", lambda v, c=c, ab=ab: v.tensor_scalar(
            phi[:], acv[:, ab, :], theta[:, c:c + 1], None, OP.mult))
        p.op("vector", lambda v: v.tensor_scalar(qs[:], phi[:], TPI, None, OP.mult), inc=True)
        vq = p.semV.n
        PI = HALF_PI * 2
        p.wait("scalar", p.semV, vq)
        sc1 = p.op("scalar", lambda s_: s_.activation(
            nsi[:, 0, :], qs[:], AF.Identity, bias=st.b128_col[:]), inc=True)
        p.wait("vector", p.semS, sc1)
        p.op("vector", lambda v: v.scalar_tensor_tensor(
            cbx[:], nsi[:, 0, :], -127.5, qs[:], OP.add, OP.is_le))
        p.op("vector", lambda v: v.scalar_tensor_tensor(
            cbx[:], nsi[:, 0, :], -128.0, cbx[:], OP.add, OP.add))
        p.op("vector", lambda v: v.scalar_tensor_tensor(
            sinb[:], cbx[:], -TWO_PI, phi[:], OP.mult, OP.add))
        p.op("vector", lambda v: v.tensor_scalar(
            sinb[:], sinb[:], PI, -PI, OP.min, OP.max))
        p.op("vector", lambda v: v.tensor_scalar(qs[:], qs[:], 0.25, None, OP.add), inc=True)
        vq2 = p.semV.n
        p.wait("scalar", p.semV, vq2)
        sc2 = p.op("scalar", lambda s_: s_.activation(
            nsi[:, 0, :], qs[:], AF.Identity, bias=st.b128_col[:]), inc=True)
        ss1 = p.op("scalar", lambda s_: s_.activation(sinb[:], sinb[:], AF.Sin), inc=True)
        p.wait("vector", p.semS, sc2)
        p.op("vector", lambda v: v.scalar_tensor_tensor(
            cbx[:], nsi[:, 0, :], -127.5, qs[:], OP.add, OP.is_le))
        p.op("vector", lambda v: v.scalar_tensor_tensor(
            cbx[:], nsi[:, 0, :], -128.0, cbx[:], OP.add, OP.add))
        p.op("vector", lambda v: v.scalar_tensor_tensor(
            cosb[:], cbx[:], -TWO_PI, phi[:], OP.mult, OP.add))
        p.op("vector", lambda v: v.tensor_scalar(
            cosb[:], cosb[:], HALF_PI, None, OP.add))
        p.op("vector", lambda v: v.tensor_scalar(
            cosb[:], cosb[:], PI, -PI, OP.min, OP.max), inc=True)
        va = p.semV.n
        p.wait("scalar", p.semV, va)
        ss2 = p.op("scalar", lambda s_: s_.activation(cosb[:], cosb[:], AF.Sin), inc=True)
        p.wait("vector", p.semS, ss2)
        p.op("vector", lambda v, c=c: v.tensor_tensor(cosb[:], cosb[:], kbb[:, c, :], OP.mult))
        p.op("vector", lambda v, c=c: v.tensor_tensor(sinb[:], sinb[:], kbb[:, c, :], OP.mult))
        p.op("vector", lambda v, c=c: v.tensor_tensor_scan(
            ncs[0][:, c, :], cosb[:], cosb[:], carry[0][:, c:c + 1], OP.add, OP.bypass))
        p.op("vector", lambda v, c=c: v.tensor_tensor_scan(
            ncs[1][:, c, :], sinb[:], sinb[:], carry[1][:, c:c + 1], OP.add, OP.bypass), inc=True)
        sq_prev = p.semV.n

    p.op("vector", lambda v: v.tensor_copy(
        carry[0][:], ncs[0][:, :, 511:512].rearrange("p c o -> p (c o)")))
    p.op("vector", lambda v: v.tensor_copy(
        carry[1][:], ncs[1][:, :, 511:512].rearrange("p c o -> p (c o)")))
    p.op("vector", lambda v: v.tensor_copy(denC[0:32, 0:1], den[0:32, 511:512]), inc=True)

    vgd = {}
    for c in range(NCH):
        ab = c % 2
        if c >= 2:
            p.wait("scalar", p.semV, vgd[c - 2])
        s0 = p.op("scalar", lambda s_, c=c, ab=ab: s_.activation(
            acv[:, ab, :], zg[:, c, 0:512], AF.Identity, scale=ckc[:, c + 8, 0:1]), inc=True)
        p.wait("vector", p.semS, s0)
        for w in range(1, 4):
            p.op("vector", lambda v, c=c, w=w, ab=ab: v.scalar_tensor_tensor(
                acv[:, ab, :], zg[:, c, w:512 + w], ckc[:, c + 8, w:w + 1], acv[:, ab, :],
                OP.mult, OP.add), inc=(w == 3))
        vgd[c] = p.semV.n
        p.wait("scalar", p.semV, vgd[c])
        p.op("scalar", lambda s_, c=c, ab=ab: s_.activation(
            zg[:, c, 0:512], acv[:, ab, :], AF.Silu), inc=True)
    sgate = p.semS.n

    SB = 4
    p.claim_bank(SB)
    first = True
    for c in range(NCH):
        for ri in range(2):
            p.wait("vector", p.semT, p.semT.n)
            vb = p.op("vector", lambda v, c=c, ri=ri: v.tensor_copy(
                nbf[ri][:], ncs[ri][:, c, :]), inc=True)
            p.wait("scalar", p.semV, vb)
            p.wait("scalar", p.semT, p.semT.n)
            sqs = p.op("scalar", lambda s_, ri=ri: s_.activation(
                sqbf[:], nbf[ri][:], AF.Square), inc=True)
            p.wait("tensor", p.semS, sqs)
            p.op("tensor", lambda t, c=c, ri=ri, first=first: t.matmul(
                psum[SB][0:32, 0:512], G32[c], sqbf[:],
                start=first, stop=(c == NCH - 1 and ri == 1)), inc=True)
            first = False
    tS2 = p.semT.n
    p.wait("vector", p.semT, tS2)
    p.op("vector", lambda v: v.tensor_copy(s2[0:32, :], psum[SB][0:32, 0:512]),
         inc_sem=p.bankV[SB], inc=True)
    p.evac_done(SB)
    if tt == int(_os.environ.get("MK_DBGT", "0")):
        dbg(st, "s2r", lambda: s2[0:32, :], [32, 512], F32)
        dbg(st, "sqbf7", lambda: sqbf[:], [128, 512], BF16)
        dbg(st, "nbf7", lambda: nbf[1][:], [128, 512], BF16)
    p.op("vector", lambda v: v.tensor_scalar(s2[0:32, :], s2[0:32, :], 1.0 / 64, None, OP.mult))
    p.op("vector", lambda v: v.tensor_scalar(clw[0:32, :], den[0:32, :], 1e-4, None, OP.max), inc=True)
    vcl = p.semV.n
    p.wait("scalar", p.semV, vcl)
    scl2 = p.op("scalar", lambda s_: s_.activation(cl2[0:32, :], clw[0:32, :], AF.Square), inc=True)
    p.wait("vector", p.semS, scl2)
    p.op("vector", lambda v: v.scalar_tensor_tensor(
        s2[0:32, :], cl2[0:32, :], 1e-5, s2[0:32, :], OP.mult, OP.add), inc=True)
    vs2 = p.semV.n
    p.wait("scalar", p.semV, vs2)
    ssq = p.op("scalar", lambda s_: s_.activation(s2[0:32, :], s2[0:32, :], AF.Sqrt), inc=True)
    p.wait("vector", p.semS, ssq)
    vrc = p.op("vector", lambda v: v.reciprocal(s2[0:32, :], s2[0:32, :]), inc=True)
    p.wait("gpsimd", p.semV, vrc)
    p.dma("gpsimd", lambda e: e.dma_start(out=L["pwd"], in_=s2[0:32, :]), sem=sPw)
    p.fence("gpsimd", sPw)
    for c in range(NCH):
        for j in range(4):
            p.dma("gpsimd", lambda e, c=c, j=j: e.dma_start(
                out=kbb[32 * j:32 * (j + 1), c, :],
                in_=L["pwd"][4 * c + j:4 * c + j + 1, :].partition_broadcast(32)), sem=sPw)
    p.fence("vector", sPw)
    p.wait("vector", p.semS, p.semS.n)

    for c in range(NCH):
        bank = 5 + (c % 2)
        p.claim_bank(bank)
        for ri in range(2):
            p.wait("vector", p.semT, p.semT.n)
            vb = p.op("vector", lambda v, c=c, ri=ri: v.tensor_copy(
                nbf[ri][:], ncs[ri][:, c, :]), inc=True)
            p.wait("tensor", p.semV, vb)
            p.op("tensor", lambda t, c=c, ri=ri, bank=bank: t.matmul(
                psum[bank][:, 0:512], wre_bd[:, ri, :], nbf[ri][:],
                start=(ri == 0), stop=(ri == 1)), inc=(ri == 1))
        tv = p.semT.n
        p.wait("vector", p.semT, tv)
        p.op("vector", lambda v, c=c, bank=bank: v.tensor_tensor(
            acv[:, 0, :], psum[bank][:, 0:512], kbb[:, c, :], OP.mult),
            inc_sem=p.bankV[bank])
        p.evac_done(bank)
        p.op("vector", lambda v, c=c: v.tensor_tensor(
            prod1[:, c, :], acv[:, 0, :], zg[:, c, 0:512], OP.mult), inc=(c == NCH - 1))
    vy = p.semV.n

    if tt == int(_os.environ.get("MK_DBGT", "0")):
        dbg(st, "pw0", lambda: p_w[0:32, :], [32, 512], F32)
        dbg(st, "den0", lambda: den[0:32, :], [32, 512], F32)
        dbg(st, "scl0", lambda: s2[0:32, :], [32, 512], F32)
        dbg(st, "ncs0", lambda: ncs[0][:], [128, NCH, 512], F32)
        dbg(st, "prod10", lambda: prod1[:], [128, NCH, 512], BF16)
        dbg(st, "zg0", lambda: zg[:, :, 0:512], [128, NCH, 512], BF16)
        dbg(st, "zv0", lambda: zv[:, :, 0:515], [128, NCH, 515], F32)

    p.wait("tensor", p.semV, vy)
    xo = st.xout.rearrange("(c p) t -> p c t", p=128)
    for mc in range(NCH):
        bank = mc % 4
        p.claim_bank(bank)
        for kc in range(NCH):
            p.op("tensor", lambda t, kc=kc, mc=mc, bank=bank: t.matmul(
                psum[bank][:, 0:512], w_out[:, kc, mc * 128:(mc + 1) * 128],
                prod1[:, kc, :], start=(kc == 0), stop=(kc == NCH - 1)), inc=(kc == NCH - 1))
        tv = p.semT.n
        ab = mc % 2
        p.wait("vector", p.semT, tv)
        p.wait("vector", aosem[ab], aosem[ab].n)
        p.op("vector", lambda v, ab=ab, bank=bank: v.tensor_copy(
            aout[ab][:], psum[bank][:, 0:512]), inc_sem=p.bankV[bank], inc=True)
        p.evac_done(bank)
        va = p.semV.n
        p.wait("gpsimd", p.semV, va)
        p.dma("gpsimd", lambda e, mc=mc, ab=ab, t0=t0: e.dma_start(
            out=xo[:, mc, t0:t0 + 512], in_=aout[ab][:], accum_op=OP.add), sem=aosem[ab])




def ph_l1_ffn(st):
    nc, p, sb = st.nc, st.p, st.sb
    psum, psT = st.psum, st.psT
    xn = st.xn
    W = st.W

    st.push_region()
    sE = [p.gsem("sE0"), p.gsem("sE1"), p.gsem("sE2"), p.gsem("sE3")]
    sCw = p.gsem("sCw")
    sMo = [p.gsem("sMo0"), p.gsem("sMo1")]
    cwd = nc.dram_tensor("cwd", [16, T], BF16, kind="Internal").ap()

    rw_sb = sb("rw_sb", [128, NCH, 16], BF16)
    rb_bc = sb("rb_bc", [128, 8, 16])
    eb1c = sb("eb1c", [128, 16, 8])
    ebsb = sb("ebsb", [16, D], BF16)
    rtm = sb("rtm", [128, 8, 16])
    mx8 = sb("mx8", [128, 8])
    rcp8 = sb("rcp8", [128, 8])
    t8 = sb("t8", [128, 8, 8])
    msk8 = sb("msk8", [128, 8, 16])
    tmp8 = sb("tmp8", [128, 8, 16])
    cwtm = sb("cwtm", [128, 8, 128], BF16)
    cwT = sb("cwT", [128, 8, 128], BF16)
    cw_bf = sb("cw_bf", [16, 1024], BF16)
    cwb = sb("cwb", [128, 1024])
    ew1 = [sb(f"ew1_{i}", [128, NCH, D], BF16) for i in range(2)]
    ew2 = [sb(f"ew2_{i}", [128, NCH, D], BF16) for i in range(2)]
    h_q = sb("h_q", [128, NCH, 1024], BF16)
    hrt = sb("hrt", [128, 512])
    acc = sb("accm", [128, NCH, 1024])

    p.dma("gpsimd", lambda e: e.dma_start(out=rw_sb[:], in_=W["router_W"].rearrange("(c p) m -> p c m", p=128)), sem=sCw)
    for j in range(8):
        p.dma("gpsimd", lambda e, j=j: e.dma_start(
            out=rb_bc[:, j, 0:15], in_=W["router_b"].partition_broadcast(128)), sem=sCw)
    p.dma("gpsimd", lambda e: e.dma_start(
        out=eb1c[:], in_=W["eb1sb1"].rearrange("e p c -> p e c")), sem=sCw)
    p.dma("gpsimd", lambda e: e.dma_start(out=ebsb[:], in_=W["eb2sb2"]), sem=sCw)
    p.mset("vector", lambda v: v.memset(rb_bc[:, :, 15:16], 0.0))
    p.mset("vector", lambda v: v.memset(cwtm[:], 0.0))

    def ew_src(e, which):
        if e >= 16:
            return None
        if which == 0:
            base = W["eW1"][e] if e < 15 else W["sW1"]
        else:
            base = W["eW2"][e] if e < 15 else W["sW2"]
        return base.rearrange("(c p) m -> p c m", p=128)

    t_mm1_end = {}
    t_mm2_end = {}
    p.dma("gpsimd", lambda e: e.dma_start(out=ew1[0][:], in_=ew_src(0, 0)), sem=sE[0])
    p.dma("gpsimd", lambda e: e.dma_start(out=ew2[0][:], in_=ew_src(0, 1)), sem=sE[2])

    for half in range(2):
        h0 = half * 1024
        RB = 6
        p.claim_bank(RB)
        if half == 0:
            p.wait("tensor", p.semV, st.xn_done)
            p.fence("tensor", sCw)
        for tb in range(8):
            for kc in range(NCH):
                p.op("tensor", lambda t, tb=tb, kc=kc, h0=h0: t.matmul(
                    psum[RB][:, tb * 16:(tb + 1) * 16],
                    xn[:, kc, 1 + h0 + tb * 128:1 + h0 + (tb + 1) * 128],
                    rw_sb[:, kc, :], start=(kc == 0), stop=(kc == NCH - 1)),
                    inc=(tb == 7 and kc == NCH - 1))
        tv = p.semT.n
        p.wait("vector", p.semT, tv)
        p.fence("vector", sCw)
        p.op("vector", lambda v: v.tensor_tensor(
            rtm[:], psum[RB][:, 0:128].rearrange("p (b e) -> p b e", e=16),
            rb_bc[:], OP.add), inc_sem=p.bankV[RB], inc=True)
        p.evac_done(RB)
        p.op("vector", lambda v: v.tensor_reduce(
            mx8[:], rtm[:, :, 0:15], AX, OP.max), inc=True)
        p.op("vector", lambda v: v.tensor_tensor(
            rtm[:], rtm[:], mx8[:, :, None].broadcast_to([128, 8, 16]),
            OP.subtract), inc=True)
        vsub = p.semV.n
        p.wait("scalar", p.semV, vsub)
        sex = p.op("scalar", lambda s_: s_.activation(rtm[:, :, 0:15], rtm[:, :, 0:15], AF.Exp), inc=True)
        p.wait("vector", p.semS, sex)
        p.op("vector", lambda v: v.tensor_reduce(rcp8[:], rtm[:, :, 0:15], AX, OP.add))
        for j in range(8):
            p.op("vector", lambda v, j=j: v.max(
                t8[:, j, :], rtm[:, j, 0:15]))
        p.op("vector", lambda v: v.reciprocal(rcp8[:], rcp8[:]), inc=True)
        p.op("vector", lambda v: v.tensor_tensor(
            msk8[:], rtm[:], t8[:, :, 2:3].broadcast_to([128, 8, 16]), OP.is_ge))
        p.op("vector", lambda v: v.tensor_tensor(
            tmp8[:], rtm[:], rcp8[:, :, None].broadcast_to([128, 8, 16]), OP.mult))
        p.op("vector", lambda v: v.tensor_tensor(
            cwtm[:, :, 0:16], tmp8[:], msk8[:], OP.mult))
        for j in range(8):
            p.op("vector", lambda v, j=j: v.memset(cwtm[:, j, 15:16], 1.0), inc=(j == 7))
        vcw = p.semV.n
        if half == 0:
            dbg(st, "cw0", lambda: cwtm[:, :, 0:16], [128, 8, 16], BF16)
            dbg(st, "rtm0", lambda: rtm[:], [128, 8, 16], F32)
            dbg(st, "t80", lambda: t8[:], [128, 8, 8], F32)
            dbg(st, "rcp80", lambda: rcp8[:], [128, 8], F32)
        p.wait("sync", p.semV, vcw)
        sCt = p.gsem("sCt")
        for j in range(8):
            p.dma("sync", lambda e, j=j: e.dma_start_transpose(
                cwT[:, j, :], cwtm[:, j, :]), sem=sCt)
        p.fence("vector", sCt)
        vcb = p.op("vector", lambda v: v.tensor_copy(
            cw_bf[0:16, :], cwT[0:16, :, :].rearrange("p b t -> p (b t)")), inc=True)
        p.wait("sync", p.semV, vcb)
        p.dma("sync", lambda e, h0=h0: e.dma_start(
            out=cwd[:, h0:h0 + 1024], in_=cwT[0:16, :, :].rearrange("p b t -> p (b t)")), sem=sCt)
        p.fence("gpsimd", sCt)

        for e in range(16):
            eb = e % 2
            nxt = e + 1 if e < 15 else (0 if half == 0 else None)
            if nxt is not None and not (half == 1 and e == 15):
                nb = nxt % 2
                key1, key2 = (half, e - 1, 1), (half, e - 1, 2)
                p.wait("gpsimd", p.semT, t_mm1_end.get(key1, 0))
                p.dma("gpsimd", lambda ee, nxt=nxt, nb=nb: ee.dma_start(
                    out=ew1[nb][:], in_=ew_src(nxt, 0)), sem=sE[nb])
                p.wait("gpsimd", p.semT, t_mm2_end.get(key2, 0))
                p.dma("gpsimd", lambda ee, nxt=nxt, nb=nb: ee.dma_start(
                    out=ew2[nb][:], in_=ew_src(nxt, 1)), sem=sE[2 + nb])
            if e < 15:
                p.wait("gpsimd", p.semV, p.semV.n)
                p.dma("gpsimd", lambda ee, e=e, h0=h0: ee.dma_start(
                    out=cwb[:], in_=cwd[e:e + 1, h0:h0 + 1024].partition_broadcast(128)),
                    sem=sCw)
                p.fence("vector", sCw)
            p.fence("tensor", sE[eb])
            for nt in range(2):
                for mch in range(NCH):
                    bank = (mch + nt * NCH) % 2
                    p.claim_bank(bank)
                    for kc in range(NCH):
                        p.op("tensor", lambda t, mch=mch, kc=kc, nt=nt, eb=eb, h0=h0, bank=bank: t.matmul(
                            psum[bank][:, 0:512], ew1[eb][:, kc, mch * 128:(mch + 1) * 128],
                            xn[:, kc, 1 + h0 + nt * 512:1 + h0 + (nt + 1) * 512],
                            start=(kc == 0), stop=(kc == NCH - 1)), inc=(kc == NCH - 1))
                    tv = p.semT.n
                    p.wait("scalar", p.semT, tv)
                    if e < 15:
                        p.wait("scalar", p.semV, p.semV.n)
                        sr = p.op("scalar", lambda s_, mch=mch, e=e, bank=bank: s_.activation(
                            hrt[:], psum[bank][:, 0:512], AF.Relu,
                            bias=eb1c[:, e, mch:mch + 1]), inc=True)
                        p.wait("vector", p.semS, sr)
                        p.op("vector", lambda v, mch=mch, nt=nt, bank=bank: v.tensor_tensor(
                            h_q[:, mch, nt * 512:(nt + 1) * 512], hrt[:],
                            cwb[:, nt * 512:(nt + 1) * 512], OP.mult),
                            inc_sem=p.bankV[bank], inc=True)
                    else:
                        p.op("scalar", lambda s_, mch=mch, nt=nt, e=e, bank=bank: s_.activation(
                            h_q[:, mch, nt * 512:(nt + 1) * 512], psum[bank][:, 0:512],
                            AF.Relu, bias=eb1c[:, e, mch:mch + 1]),
                            inc_sem=p.bankV[bank], inc=True)
                    p.evac_done(bank)
            t_mm1_end[(half, e, 1)] = p.semT.n
            vh = p.semV.n
            sh = p.semS.n
            p.wait("tensor", p.semV, vh)
            p.wait("tensor", p.semS, sh)
            p.fence("tensor", sE[2 + eb])
            if e == 0:
                p.fence("tensor", sCw)
                p.wait("tensor", p.semV, vcb)
            for nt in range(2):
                for mc in range(NCH):
                    bank = 2 + ((mc + nt * NCH) % 2)
                    p.claim_bank(bank)
                    nmm = NCH + (1 if e == 0 else 0)
                    if e == 0:
                        p.op("tensor", lambda t, mc=mc, nt=nt, h0=h0, bank=bank: t.matmul(
                            psum[bank][:, 0:512], ebsb[0:16, mc * 128:(mc + 1) * 128],
                            cw_bf[0:16, nt * 512:(nt + 1) * 512], start=True, stop=False))
                    for kc in range(NCH):
                        p.op("tensor", lambda t, mc=mc, kc=kc, nt=nt, eb=eb, bank=bank, e=e: t.matmul(
                            psum[bank][:, 0:512], ew2[eb][:, kc, mc * 128:(mc + 1) * 128],
                            h_q[:, kc, nt * 512:(nt + 1) * 512],
                            start=(kc == 0 and e != 0), stop=(kc == NCH - 1)),
                            inc=(kc == NCH - 1))
                    tv = p.semT.n
                    p.wait("vector", p.semT, tv)
                    if e == 0:
                        p.op("vector", lambda v, mc=mc, nt=nt, bank=bank: v.tensor_copy(
                            acc[:, mc, nt * 512:(nt + 1) * 512], psum[bank][:, 0:512]),
                            inc_sem=p.bankV[bank], inc=True)
                    else:
                        p.op("vector", lambda v, mc=mc, nt=nt, bank=bank: v.tensor_tensor(
                            acc[:, mc, nt * 512:(nt + 1) * 512],
                            acc[:, mc, nt * 512:(nt + 1) * 512],
                            psum[bank][:, 0:512], OP.add),
                            inc_sem=p.bankV[bank], inc=True)
                    p.evac_done(bank)
            t_mm2_end[(half, e, 2)] = p.semT.n

        vacc = p.semV.n
        p.wait("gpsimd", p.semV, vacc)
        xo = st.xout.rearrange("(c p) t -> p c t", p=128)
        for mc in range(NCH):
            ab = mc % 2
            p.wait("gpsimd", sMo[ab], sMo[ab].n)
            p.dma("gpsimd", lambda e, mc=mc, h0=h0: e.dma_start(
                out=xo[:, mc, h0:h0 + 1024], in_=acc[:, mc, :], accum_op=OP.add),
                sem=sMo[ab])
        p.fence("vector", sMo[0])
        p.fence("vector", sMo[1])

    p.barrier()
    st.pop_region()



import os as _os
N_PHASES = int(_os.environ.get("MK_PHASES", "99"))
SKIP_PH = set(x for x in _os.environ.get("MK_SKIP", "").split(",") if x)
DEBUG_NAMES = tuple(x for x in _os.environ.get("MK_DEBUG", "").split(",") if x)


def build_program(n_phases=None, debug_names=None):
    n_phases = N_PHASES if n_phases is None else n_phases
    debug_names = DEBUG_NAMES if debug_names is None else debug_names
    st = St()
    setup(st, debug_names)
    p, sb = st.p, st.sb

    st.xn = sb("xn", [128, NCH, T + 2], BF16)
    if n_phases >= 1:
        ph_layernorm(st, st.x0, st.xn, write_through=True)
    if "xn" in st.debug_names:
        dbg(st, "xn", lambda: st.xn[:, :, 1:T + 1], [128, NCH, T], BF16)
    if "xnal" in st.debug_names:
        dbg(st, "xnal", lambda: st.xn[:, :, 0:T], [128, NCH, T], BF16)
    if n_phases >= 2 and "attn" not in SKIP_PH:
        ph_l0_attn(st)
    if n_phases >= 3 and "ffn" not in SKIP_PH:
        ph_l0_ffn(st)
    if n_phases >= 4:
        ph_layernorm(st, st.xout, st.xn, write_through=False)
        if "xn2" in st.debug_names:
            dbg(st, "xn2", lambda: st.xn[:, :, 1:T + 1], [128, NCH, T], BF16)
    if n_phases >= 4 and "attn1" not in SKIP_PH:
        ph_l1_attn(st)
    if n_phases >= 5 and "ffn1" not in SKIP_PH:
        ph_l1_ffn(st)

    p.barrier()
    p.replay()
    return st


def kernel(**inputs):
    g = host_layout(inputs)
    xs = np.ascontiguousarray(np.asarray(inputs["L0_attn_hidden_states"], dtype=np.float32))
    st = build_program()
    in_maps = []
    for c in range(8):
        m = {"x0": np.ascontiguousarray(xs[c].T)}
        m.update(g)
        in_maps.append(m)
    res = run_bass_kernel_spmd(st.nc, in_maps, core_ids=list(range(8)))
    out = np.stack([np.asarray(r["xout"]).T for r in res.results], axis=0)
    return out.astype(np.float32)
